# version 05
# baseline (speedup 1.0000x reference)
import os
import numpy as np
import ml_dtypes
from contextlib import ExitStack

import concourse.bass as bass
import concourse.mybir as mybir
from concourse.alu_op_type import AluOpType as OP
from concourse.bass_utils import run_bass_kernel_spmd

F32 = mybir.dt.float32
BF16 = mybir.dt.bfloat16
I32 = mybir.dt.int32
AF = mybir.ActivationFunctionType
AX = mybir.AxisListType
BFNP = ml_dtypes.bfloat16

T = 2048
D = 4096
NT = T // 128
NF = D // 128
NSLAB, SLAB = 4, 512
C0 = 8.0
EPS = 1e-5
ISQ = float(1.0 / np.sqrt(128.0))
RG = [[0, 1], [2, 3], [4, 5], [6, 7]]


class Sem:
    def __init__(self, ctx, nc, name):
        self.h = ctx.enter_context(nc.semaphore(name))
        self.n = 0

    def inc(self, instr, k=1, expect=None):
        instr.then_inc(self.h, k)
        self.n += k
        if expect is not None:
            assert self.n == expect, f"sem mark drift: {self.n} != {expect}"
        return self.n

    def w(self, eng, val=None):
        eng.wait_ge(self.h, self.n if val is None else val)


def cv(apx, c=NF):
    return apx.rearrange("p (c t) -> p c t", c=c)


def build(n_sub=4):
    nc = bass.Bass("TRN2", target_bir_lowering=False)

    ap = lambda t: t.ap()
    x0 = ap(nc.dram_tensor("x0", [T, D], F32, kind="ExternalInput"))
    wq_t = ap(nc.dram_tensor("wq_t", [D, D], F32, kind="ExternalInput"))
    wk_t = ap(nc.dram_tensor("wk_t", [D, D], F32, kind="ExternalInput"))
    wv_t = ap(nc.dram_tensor("wv_t", [D, D], F32, kind="ExternalInput"))
    bq_r = ap(nc.dram_tensor("bq_r", [128, NF], F32, kind="ExternalInput"))
    bk_r = ap(nc.dram_tensor("bk_r", [128, NF], F32, kind="ExternalInput"))
    bv_r = ap(nc.dram_tensor("bv_r", [128, NF], F32, kind="ExternalInput"))
    wqkv_t = ap(nc.dram_tensor("wqkv_t", [D, 6144], F32, kind="ExternalInput"))
    wd_t = ap(nc.dram_tensor("wd_t", [D, D], F32, kind="ExternalInput"))
    qv_t = ap(nc.dram_tensor("qv_t", [D, D], I32, kind="ExternalInput"))
    qs_t = ap(nc.dram_tensor("qs_t", [128, D], F32, kind="ExternalInput"))
    fb_r = ap(nc.dram_tensor("fb_r", [128, NF], F32, kind="ExternalInput"))
    maskT = ap(nc.dram_tensor("maskT", [2 * T, T], BF16, kind="ExternalInput"))
    y = ap(nc.dram_tensor("y", [T, D], F32, kind="ExternalOutput"))

    xs0 = ap(nc.dram_tensor("xs0", [T, D], F32))
    xs1 = ap(nc.dram_tensor("xs1", [T, D], F32))
    u_dram = ap(nc.dram_tensor("u_dram", [T, D], F32))
    qa_slab = ap(nc.dram_tensor("qa_slab", [NSLAB, D, SLAB], BF16))
    ka_slab = ap(nc.dram_tensor("ka_slab", [NSLAB, D, SLAB], BF16))
    va_slab = ap(nc.dram_tensor("va_slab", [NSLAB, D, SLAB], BF16))
    qb_dram = ap(nc.dram_tensor("qb_dram", [D, T], BF16))
    kvb_loc = ap(nc.dram_tensor("kvb_loc", [T, T], BF16))
    kvb_gath = ap(nc.dram_tensor("kvb_gath", [2 * T, T], BF16))
    ctxbT = ap(nc.dram_tensor("ctxbT", [D, T], BF16))
    wcache = ap(nc.dram_tensor("wcache", [NF, 128, D], BF16))
    dbg = {}
    if os.environ.get("KDBG"):
        dbg["d_gath"] = ap(nc.dram_tensor("d_gath", [2 * T, T], BF16, kind="ExternalOutput"))
        dbg["d_ctxb"] = ap(nc.dram_tensor("d_ctxb", [D, T], BF16, kind="ExternalOutput"))
        dbg["d_qb"] = ap(nc.dram_tensor("d_qb", [D, T], BF16, kind="ExternalOutput"))
        dbg["d_u"] = ap(nc.dram_tensor("d_u", [T, D], F32, kind="ExternalOutput"))

    est = ExitStack()
    with est:
        E = est.enter_context
        sm = {k: Sem(est, nc, "s_" + k) for k in [
            "VC", "AC", "PC", "PE", "TR",
            "x", "u", "w0", "qk", "msk", "aQ",
            "xst", "ust", "evst", "cc", "aux",
        ]}
        VC, ACh, PEs, TRs = sm["VC"], sm["AC"], sm["PE"], sm["TR"]

        id_bf = E(nc.sbuf_tensor("id_bf", [128, 128], BF16))
        id_f32 = E(nc.sbuf_tensor("id_f32", [128, 128], F32))
        idv = E(nc.sbuf_tensor("idv", [128, 128], I32))
        bq_sb = E(nc.sbuf_tensor("bq_sb", [128, NF], F32))
        bk_sb = E(nc.sbuf_tensor("bk_sb", [128, NF], F32))
        fb_sb = E(nc.sbuf_tensor("fb_sb", [128, NF], F32))
        bv_sb = E(nc.sbuf_tensor("bv_sb", [128, NF], F32))
        c0b = E(nc.sbuf_tensor("c0b", [128, 1], F32))
        zrb = E(nc.sbuf_tensor("zrb", [128, 1], F32))
        epb = E(nc.sbuf_tensor("epb", [128, 1], F32))

        with nc.Block() as blk:
            @blk.sync
            def _(sy):
                sm["aux"].inc(sy.dma_start(out=bq_sb[:], in_=bq_r), 16)
                sm["aux"].inc(sy.dma_start(out=bk_sb[:], in_=bk_r), 16)
                sm["aux"].inc(sy.dma_start(out=fb_sb[:], in_=fb_r), 16)
                sm["aux"].inc(sy.dma_start(out=bv_sb[:], in_=bv_r), 16)

            @blk.gpsimd
            def _(g):
                sm["PC"].inc(g.iota(idv[:], pattern=[[-1, 128]], base=0,
                                    channel_multiplier=1), 1)

            @blk.vector
            def _(v):
                sm["PC"].w(v)
                sm["aux"].w(v)
                VC.inc(v.tensor_scalar(id_bf[:], idv[:], 0.0, None, OP.is_equal), 1)
                VC.inc(v.tensor_scalar(id_f32[:], idv[:], 0.0, None, OP.is_equal), 1)
                VC.inc(v.memset(c0b[:], -C0), 1)
                VC.inc(v.memset(zrb[:], 0.0), 1)
                VC.inc(v.memset(epb[:], EPS), 1)

        xsrc = {1: x0, 2: x0, 3: xs0, 4: xs1, 5: xs0}
        xdst = {2: xs0, 3: xs1, 4: xs0}

        def emit_boundary(k, xnT=None):
            last = (k == n_sub + 1)
            src = xsrc[k]
            dst = y if last else xdst.get(k)
            with ExitStack() as ls, nc.Block() as blk:
                L = ls.enter_context
                z_sb = [L(nc.sbuf_tensor(f"z{k}_{i}", [128, D], F32)) for i in range(2)]
                u_sb = [L(nc.sbuf_tensor(f"ubf{k}", [128, D], F32))]
                xn_bf = [L(nc.sbuf_tensor(f"xnb{k}_{i}", [128, D], BF16)) for i in range(2)]
                bnst = L(nc.sbuf_tensor(f"bnst{k}", [128, 48], F32))
                mv = L(nc.sbuf_tensor(f"mv{k}", [128, 2], F32))
                lnv = L(nc.sbuf_tensor(f"lnv{k}", [128, 1], F32))
                rstd = L(nc.sbuf_tensor(f"rstd{k}", [128, 1], F32))
                negmr = L(nc.sbuf_tensor(f"negmr{k}", [128, 1], F32))
                ps_tr = [L(nc.psum_tensor(f"pstr{k}_{i}", [128, 1024], BF16)) for i in range(2)]

                ust0, xst0 = sm["ust"].n, sm["xst"].n
                xb, ub0, stb = sm["x"].n, sm["u"].n, sm["xst"].n
                vcb, acb, trb = VC.n, ACh.n, TRs.n

                if not last:
                    def vstart(tt):
                        return vcb if tt == 0 else vcb + 12 + 16 * (tt - 1)
                    ev_m = [[0] * 4 for _ in range(NT)]
                    zadd_m = [0] * NT
                    aggr_m = [0] * NT
                    xn_m = [0] * NT
                    for tt in range(NT):
                        s0_ = vstart(tt)
                        o = 0
                        if tt >= 1:
                            for r in range(4):
                                ev_m[tt - 1][r] = s0_ + o + 1
                                o += 1
                        zadd_m[tt] = s0_ + o + 1; o += 1
                        o += 8
                        aggr_m[tt] = s0_ + o + 1; o += 1
                        o += 1
                        xn_m[tt] = s0_ + o + 1; o += 1
                    fs = vstart(NT)
                    for r in range(4):
                        ev_m[NT - 1][r] = fs + r + 1
                    rstd_m = [acb + 2 * (tt + 1) for tt in range(NT)]
                    tr_m = [[trb + 4 * tt + r + 1 for r in range(4)] for tt in range(NT)]
                else:
                    zadd_m = [vcb + tt + 1 for tt in range(NT)]
                    ev_m = xn_m = aggr_m = rstd_m = tr_m = None

                @blk.sync
                def _(sy):
                    sm["ust"].w(sy, ust0)
                    sm["xst"].w(sy, xst0)
                    for tt in range(NT):
                        b = tt % 2
                        if tt >= 2:
                            VC.w(sy, (zadd_m if last else xn_m)[tt - 2])
                            if dst is not None:
                                sy.wait_ge(sm["xst"].h, stb + 16 * (tt - 1))
                        sm["x"].inc(sy.dma_start(out=z_sb[b][:],
                                                 in_=src[tt * 128:(tt + 1) * 128, :]), 16)
                        if k >= 2:
                            if tt >= 1:
                                VC.w(sy, zadd_m[tt - 1])
                            sm["u"].inc(sy.dma_start(out=u_sb[0][:],
                                                     in_=u_dram[tt * 128:(tt + 1) * 128, :]), 16)

                @blk.vector
                def _(v):
                    for tt in range(NT):
                        b = tt % 2
                        if not last and tt >= 1:
                            for r in range(4):
                                TRs.w(v, tr_m[tt - 1][r])
                                dst_v = cv(xnT[:])[:, r * 8:(r + 1) * 8,
                                                   (tt - 1) * 128:tt * 128]
                                VC.inc(v.tensor_copy(dst_v, cv(ps_tr[r % 2][:], c=8)),
                                       1, expect=ev_m[tt - 1][r])
                        v.wait_ge(sm["x"].h, xb + 16 * (tt + 1))
                        if k >= 2:
                            v.wait_ge(sm["u"].h, ub0 + 16 * (tt + 1))
                            VC.inc(v.tensor_tensor(z_sb[b][:], z_sb[b][:],
                                                   u_sb[0][:], OP.add), 1,
                                   expect=zadd_m[tt])
                        else:
                            VC.inc(v.nop(), 1, expect=zadd_m[tt])
                        if not last:
                            for i in range(8):
                                VC.inc(v.bn_stats(bnst[:, 6 * i:6 * i + 6],
                                                  z_sb[b][:, 512 * i:512 * (i + 1)]), 1)
                            VC.w(v)
                            VC.inc(v.bn_aggr(mv[:], bnst[:]), 1, expect=aggr_m[tt])
                            ACh.w(v, rstd_m[tt])
                            nm = VC.inc(v.tensor_scalar(negmr[:], mv[:, 0:1], rstd[:],
                                                        -1.0, OP.mult, OP.mult), 1)
                            VC.w(v, nm)
                            VC.inc(v.tensor_scalar(xn_bf[b][:], z_sb[b][:], rstd[:],
                                                   negmr[:], OP.mult, OP.add), 1,
                                   expect=xn_m[tt])
                    if not last:
                        for r in range(4):
                            TRs.w(v, tr_m[NT - 1][r])
                            dst_v = cv(xnT[:])[:, r * 8:(r + 1) * 8,
                                               (NT - 1) * 128:NT * 128]
                            VC.inc(v.tensor_copy(dst_v, cv(ps_tr[r % 2][:], c=8)),
                                   1, expect=ev_m[NT - 1][r])

                @blk.scalar
                def _(s):
                    if last:
                        return
                    for tt in range(NT):
                        VC.w(s, aggr_m[tt])
                        c1 = ACh.inc(s.activation(lnv[:], mv[:, 1:2], AF.Ln,
                                                  bias=epb[:], scale=1.0), 1)
                        ACh.w(s, c1)
                        ACh.inc(s.activation(rstd[:], lnv[:], AF.Exp,
                                             bias=zrb[:], scale=-0.5), 1,
                                expect=rstd_m[tt])

                @blk.tensor
                def _(t):
                    if last:
                        return
                    for tt in range(NT):
                        b = tt % 2
                        VC.w(t, xn_m[tt])
                        for r in range(4):
                            pb = r % 2
                            if tt * 4 + r >= 2:
                                pr_tt, pr_r = divmod(tt * 4 + r - 2, 4)
                                VC.w(t, ev_m[pr_tt][pr_r])
                            for c8 in range(8):
                                c = r * 8 + c8
                                ins = t.transpose(ps_tr[pb][:, c8 * 128:(c8 + 1) * 128],
                                                  xn_bf[b][:, c * 128:(c + 1) * 128],
                                                  id_bf[:])
                            TRs.inc(ins, 1, expect=tr_m[tt][r])

                @blk.gpsimd
                def _(g):
                    if dst is None:
                        return
                    for tt in range(NT):
                        VC.w(g, zadd_m[tt])
                        sm["xst"].inc(g.dma_start(out=dst[tt * 128:(tt + 1) * 128, :],
                                                  in_=z_sb[tt % 2][:]), 16)
                    if last:
                        sm["xst"].w(g)

        def emit_proj(w_ap, jobs, tag, xnT=None):
            n = len(jobs)
            nd_prefix = [0]
            for j in jobs:
                nd_prefix.append(nd_prefix[-1] + len(j["stores"]))
            with ExitStack() as ls, nc.Block() as blk:
                L = ls.enter_context
                wstage = [L(nc.sbuf_tensor(f"wst{tag}{i}", [128, D], F32)) for i in range(2)]
                wbf = [L(nc.sbuf_tensor(f"wbf{tag}{i}", [128, D], BF16)) for i in range(2)]
                pstage = [L(nc.sbuf_tensor(f"pst{tag}{i}", [128, T], BF16)) for i in range(2)]
                psA = [L(nc.psum_tensor(f"psA{tag}{i}", [128, 2048], F32)) for i in range(2)]

                w0b, peb, acb, stb, vcb = (sm["w0"].n, PEs.n, ACh.n, sm["evst"].n, VC.n)
                cast_m = [vcb + i + 1 for i in range(n)]

                @blk.sync
                def _(sy):
                    for i in range(n):
                        b = i % 2
                        if i >= 2:
                            VC.w(sy, cast_m[i - 2])
                        oc = jobs[i]["col"]
                        src = w_ap[:, oc * 128:(oc + 1) * 128].rearrange(
                            "(c p) j -> p c j", p=128)
                        sm["w0"].inc(sy.dma_start(out=cv(wstage[b][:]), in_=src), 16)

                @blk.vector
                def _(v):
                    for i in range(n):
                        b = i % 2
                        v.wait_ge(sm["w0"].h, w0b + 16 * (i + 1))
                        if i >= 2:
                            PEs.w(v, peb + i - 1)
                        VC.inc(v.tensor_copy(wbf[b][:], wstage[b][:]), 1,
                               expect=cast_m[i])

                @blk.tensor
                def _(t):
                    for i in range(n):
                        b = i % 2
                        VC.w(t, cast_m[i])
                        if i >= 2:
                            ACh.w(t, acb + 4 * (i - 1))
                        for fc in range(NF):
                            for ts in range(4):
                                ins = t.matmul(psA[b][:, ts * 512:(ts + 1) * 512],
                                               lhsT=wbf[b][:, fc * 128:(fc + 1) * 128],
                                               rhs=cv(xnT[:])[:, fc, ts * 512:(ts + 1) * 512],
                                               start=(fc == 0), stop=(fc == NF - 1))
                        PEs.inc(ins, 1, expect=peb + i + 1)

                @blk.scalar
                def _(s):
                    for i, j in enumerate(jobs):
                        b = i % 2
                        PEs.w(s, peb + i + 1)
                        if i >= 2:
                            s.wait_ge(sm["evst"].h, stb + 16 * nd_prefix[i - 1])
                        for ts in range(4):
                            ACh.inc(s.activation(
                                pstage[b][:, ts * 512:(ts + 1) * 512],
                                psA[b][:, ts * 512:(ts + 1) * 512], AF.Identity,
                                bias=(j["bias"] if j["bias"] is not None else zrb[:]),
                                scale=1.0), 1)

                @blk.gpsimd
                def _(g):
                    for i, j in enumerate(jobs):
                        b = i % 2
                        ACh.w(g, acb + 4 * (i + 1))
                        for (out_ap, lo, hi) in j["stores"]:
                            sm["evst"].inc(g.dma_start(out=out_ap,
                                                       in_=pstage[b][:, lo:hi]), 16)

        def emit_mix():
            with ExitStack() as ls, nc.Block() as blk:
                L = ls.enter_context
                qs_sb = L(nc.sbuf_tensor("qs_sb", [128, NF * SLAB], BF16))
                ks_sb = L(nc.sbuf_tensor("ks_sb", [128, NF * SLAB], BF16))
                vs_sb = L(nc.sbuf_tensor("vs_sb", [128, NF * SLAB], BF16))
                p4 = L(nc.sbuf_tensor("p4", [128, 256], BF16))
                p4t = L(nc.sbuf_tensor("p4t", [128, 256], BF16))
                zt = L(nc.sbuf_tensor("zt", [128, 8], F32))
                rz = L(nc.sbuf_tensor("rz", [128, 8], F32))
                vt_sb = L(nc.sbuf_tensor("vt_sb", [128, 1024], BF16))
                ctx_acc = [L(nc.sbuf_tensor(f"ctxacc{i}", [128, D], F32)) for i in range(2)]
                ps_s = L(nc.psum_tensor("ps_s", [128, 256], F32))
                ps_v = L(nc.psum_tensor("ps_v", [128, 1024], BF16))
                ps_c = [L(nc.psum_tensor(f"ps_c{i}", [128, 512], F32)) for i in range(2)]

                qkb, ustb, evstb = sm["qk"].n, sm["ust"].n, sm["evst"].n
                NSG = SLAB // 32
                NIT = NSLAB * NSG
                peb, acb, vcb = PEs.n, ACh.n, VC.n

                A_m = [peb + 3 * it + 1 for it in range(NIT)]
                C_m = [peb + 3 * it + 2 for it in range(NIT)]
                D_m = [peb + 3 * it + 3 for it in range(NIT)]
                exp_m = [acb + it + 1 for it in range(NIT)]
                red_m = [0] * NIT; rec_m = [0] * NIT
                ptr_m = [0] * NIT; vte_m = [0] * NIT
                ctx_m = [[0] * 8 for _ in range(NIT)]
                cnt = vcb
                for it in range(NIT):
                    if it >= 1:
                        for g in range(8):
                            cnt += 1; ctx_m[it - 1][g] = cnt
                    cnt += 1; red_m[it] = cnt
                    cnt += 1; rec_m[it] = cnt
                    cnt += 1; ptr_m[it] = cnt
                    cnt += 1; vte_m[it] = cnt
                for g in range(8):
                    cnt += 1; ctx_m[NIT - 1][g] = cnt

                @blk.sync
                def _(sy):
                    sm["evst"].w(sy, evstb)
                    for sl in range(NSLAB):
                        if sl >= 1:
                            PEs.w(sy, C_m[sl * NSG - 1])
                        for dst_sb, slab in ((qs_sb, qa_slab), (ks_sb, ka_slab),
                                             (vs_sb, va_slab)):
                            src = slab[sl].rearrange("(c p) t -> p c t", p=128)
                            sm["qk"].inc(sy.dma_start(out=cv(dst_sb[:]), in_=src), 16)

                @blk.tensor
                def _(t):
                    for sl in range(NSLAB):
                        t.wait_ge(sm["qk"].h, qkb + 48 * (sl + 1))
                        qv_ = cv(qs_sb[:], c=NF).rearrange("p c t -> p t c")
                        kv_ = cv(ks_sb[:], c=NF).rearrange("p c t -> p t c")
                        vv_ = cv(vs_sb[:], c=NF).rearrange("p c t -> p t c")
                        for sg in range(NSG):
                            it = sl * NSG + sg
                            t0 = sg * 32
                            if it >= 1:
                                ACh.w(t, exp_m[it - 1])
                            for g in range(8):
                                for i in range(4):
                                    tk = t0 + g * 4 + i
                                    ins = t.matmul(ps_s[32 * i:32 * i + 32,
                                                        g * 32:(g + 1) * 32],
                                                   lhsT=qv_[:, tk, :], rhs=kv_[:, tk, :],
                                                   start=True, stop=True,
                                                   tile_position=(0, 32 * i))
                            PEs.inc(ins, 1, expect=A_m[it])
                            if it >= 1:
                                VC.w(t, vte_m[it - 1])
                            for g in range(8):
                                for i in range(4):
                                    tk = t0 + g * 4 + i
                                    ins = t.transpose(ps_v[32 * i:32 * i + 32,
                                                           g * 128:(g + 1) * 128],
                                                      vv_[:, tk, :], id_bf[:],
                                                      tile_position=(0, 32 * i))
                            PEs.inc(ins, 1, expect=C_m[it])
                            VC.w(t, ptr_m[it])
                            VC.w(t, vte_m[it])
                            if it >= 1:
                                VC.w(t, ctx_m[it - 1][7])
                            for g in range(8):
                                bk = g // 4
                                off = (g % 4) * 128
                                for i in range(4):
                                    ins = t.matmul(ps_c[bk][32 * i:32 * i + 32,
                                                           off:off + 128],
                                                   lhsT=p4t[32 * i:32 * i + 32,
                                                            g * 32:(g + 1) * 32],
                                                   rhs=vt_sb[32 * i:32 * i + 32,
                                                             g * 128:(g + 1) * 128],
                                                   start=True, stop=True,
                                                   tile_position=(32 * i, 32 * i))
                            PEs.inc(ins, 1, expect=D_m[it])

                @blk.scalar
                def _(s):
                    for it in range(NIT):
                        PEs.w(s, A_m[it])
                        if it >= 1:
                            VC.w(s, ptr_m[it - 1])
                        ACh.inc(s.activation(p4[:], ps_s[:], AF.Exp,
                                             bias=c0b[:], scale=ISQ), 1,
                                expect=exp_m[it])

                def emit_ctx_evac(v, it):
                    PEs.w(v, D_m[it])
                    VC.w(v, rec_m[it])
                    ca = ctx_acc[(it // 4) % 2]
                    if it % 4 == 0 and it >= 8:
                        sm["ust"].w(v, ustb + 16 * (it // 4 - 1))
                    for g in range(8):
                        bk = g // 4
                        off = (g % 4) * 128
                        dsl = ((it % 4) * 8 + g) * 128
                        VC.inc(v.tensor_scalar(
                            ca[:, dsl:dsl + 128], ps_c[bk][:, off:off + 128],
                            rz[:, g:g + 1], None, OP.mult), 1,
                            expect=ctx_m[it][g])

                @blk.vector
                def _(v):
                    for it in range(NIT):
                        if it >= 1:
                            emit_ctx_evac(v, it - 1)
                        ACh.w(v, exp_m[it])
                        VC.inc(v.tensor_reduce(zt[:], cv(p4[:], c=8), AX.X, OP.add),
                               1, expect=red_m[it])
                        VC.w(v, red_m[it])
                        VC.inc(v.reciprocal(rz[:], zt[:]), 1, expect=rec_m[it])
                        if it >= 1:
                            PEs.w(v, D_m[it - 1])
                        VC.inc(v.transpose(p4t[:], p4[:]), 1, expect=ptr_m[it])
                        PEs.w(v, C_m[it])
                        VC.inc(v.tensor_copy(vt_sb[:], ps_v[:]), 1, expect=vte_m[it])
                    emit_ctx_evac(v, NIT - 1)

                @blk.gpsimd
                def _(g_):
                    for w in range(NIT // 4):
                        VC.w(g_, ctx_m[4 * w + 3][7])
                        ca = ctx_acc[w % 2]
                        out_apx = u_dram[w * 128:(w + 1) * 128, :].rearrange(
                            "(s g i) (h d) -> (i h) (s g) d", s=4, g=8, i=4, h=32)
                        in_apx = ca[:].rearrange("p (b d) -> p b d", b=32)
                        sm["ust"].inc(g_.dma_start(out=out_apx, in_=in_apx), 16)


        def emit_fmout(mode, tag, xnT=None, w_ap=None, bias_sb=None):
            with ExitStack() as ls, nc.Block() as blk:
                L = ls.enter_context
                wocbf = [L(nc.sbuf_tensor(f"woc{tag}{i}", [128, D], BF16)) for i in range(2)]
                if mode == "dequant":
                    qvst = [L(nc.sbuf_tensor(f"qvs{tag}0", [128, D], I32))]
                    sclb = L(nc.sbuf_tensor(f"scl{tag}", [128, D], F32))
                if mode == "w32":
                    wstage = [L(nc.sbuf_tensor(f"wsg{tag}{i}", [128, D], F32)) for i in range(2)]
                ystage = [L(nc.sbuf_tensor(f"yst{tag}0", [128, T], F32))]
                ustage = [L(nc.sbuf_tensor(f"usg{tag}{i}", [128, T], F32)) for i in range(2)]
                psA = [L(nc.psum_tensor(f"psF{tag}{i}", [128, 2048], F32)) for i in range(2)]

                w0b, peb, acb, trb, ustb = (sm["w0"].n, PEs.n, ACh.n, TRs.n, sm["ust"].n)
                vcb = VC.n
                per = {"dequant": 5, "cache": 4, "w32": 5}[mode]
                dq_m = [0] * NF
                uev_m = [[0] * 4 for _ in range(NF)]
                for oc in range(NF):
                    base = vcb + per * oc
                    o = 1 if mode != "cache" else 0
                    if mode != "cache":
                        dq_m[oc] = base + 1
                    for r in range(4):
                        uev_m[oc][r] = base + o + r + 1
                ns_per = 2 if mode == "dequant" else 1
                nload = {"dequant": 80, "cache": 16, "w32": 16}[mode]

                @blk.sync
                def _(sy):
                    for oc in range(NF):
                        b = oc % 2
                        if mode == "dequant":
                            if oc >= 1:
                                VC.w(sy, dq_m[oc - 1])
                            src = qv_t[:, oc * 128:(oc + 1) * 128].rearrange(
                                "(c p) j -> p c j", p=128)
                            sm["w0"].inc(sy.dma_start(out=cv(qvst[0][:]), in_=src), 16)
                            if oc >= 1:
                                VC.w(sy, dq_m[oc - 1])
                            qsv = qs_t.rearrange("(c g) j -> g c j", g=4)
                            for g in range(4):
                                src2 = qsv[g, :, oc * 128:(oc + 1) * 128
                                           ].partition_broadcast(32)
                                dst2 = cv(sclb[32 * g:32 * g + 32, :])
                                sm["w0"].inc(sy.dma_start(out=dst2, in_=src2), 16)
                        elif mode == "cache":
                            if oc >= 2:
                                PEs.w(sy, peb + 2 * (oc - 1))
                            sm["w0"].inc(sy.dma_start(out=wocbf[b][:], in_=wcache[oc]), 16)
                        else:
                            if oc >= 2:
                                VC.w(sy, dq_m[oc - 2])
                            src = w_ap[:, oc * 128:(oc + 1) * 128].rearrange(
                                "(c p) j -> p c j", p=128)
                            sm["w0"].inc(sy.dma_start(out=cv(wstage[b][:]), in_=src), 16)

                @blk.vector
                def _(v):
                    for oc in range(NF):
                        b = oc % 2
                        if mode == "dequant":
                            v.wait_ge(sm["w0"].h, w0b + 80 * (oc + 1))
                            if oc >= 2:
                                PEs.w(v, peb + 2 * (oc - 1))
                            VC.inc(v.scalar_tensor_tensor(
                                wocbf[b][:], qvst[0][:], -128.0, sclb[:],
                                OP.add, OP.mult), 1, expect=dq_m[oc])
                        elif mode == "w32":
                            v.wait_ge(sm["w0"].h, w0b + 16 * (oc + 1))
                            if oc >= 2:
                                PEs.w(v, peb + 2 * (oc - 1))
                            VC.inc(v.tensor_copy(wocbf[b][:], wstage[b][:]), 1,
                                   expect=dq_m[oc])
                        if oc >= 2:
                            sm["ust"].w(v, ustb + 16 * ns_per * (oc - 1))
                        for r in range(4):
                            TRs.w(v, trb + oc * 4 + r + 1)
                            VC.inc(v.tensor_copy(
                                ustage[b][:, r * 512:(r + 1) * 512],
                                psA[b][:, r * 512:(r + 1) * 512]), 1,
                                expect=uev_m[oc][r])

                @blk.tensor
                def _(t):
                    for oc in range(NF):
                        b = oc % 2
                        if mode == "cache":
                            t.wait_ge(sm["w0"].h, w0b + 16 * (oc + 1))
                        else:
                            VC.w(t, dq_m[oc])
                        if oc >= 2:
                            VC.w(t, uev_m[oc - 2][3])
                        for fc in range(NF):
                            for ts in range(4):
                                ins = t.matmul(psA[b][:, ts * 512:(ts + 1) * 512],
                                               lhsT=wocbf[b][:, fc * 128:(fc + 1) * 128],
                                               rhs=cv(xnT[:])[:, fc,
                                                              ts * 512:(ts + 1) * 512],
                                               start=(fc == 0), stop=(fc == NF - 1))
                        PEs.inc(ins, 1, expect=peb + 2 * oc + 1)
                        ACh.w(t, acb + 4 * (oc + 1))
                        for tt in range(NT):
                            r, qq = divmod(tt, 4)
                            ins = t.transpose(psA[b][:, r * 512 + qq * 128:
                                                     r * 512 + (qq + 1) * 128],
                                              ystage[0][:, tt * 128:(tt + 1) * 128],
                                              id_f32[:])
                            if qq == 3:
                                TRs.inc(ins, 1, expect=trb + oc * 4 + r + 1)
                        PEs.inc(t.nop(), 1, expect=peb + 2 * oc + 2)

                @blk.scalar
                def _(s):
                    for oc in range(NF):
                        b = oc % 2
                        PEs.w(s, peb + 2 * oc + 1)
                        if oc >= 1:
                            TRs.w(s, trb + 4 * oc)
                        for ts in range(4):
                            ACh.inc(s.activation(
                                ystage[0][:, ts * 512:(ts + 1) * 512],
                                psA[b][:, ts * 512:(ts + 1) * 512], AF.Identity,
                                bias=(bias_sb[:, oc:oc + 1] if bias_sb is not None
                                      else zrb[:]), scale=1.0), 1)

                @blk.gpsimd
                def _(g):
                    for oc in range(NF):
                        b = oc % 2
                        VC.w(g, uev_m[oc][3])
                        out_apx = u_dram[:, oc * 128:(oc + 1) * 128].rearrange(
                            "(tt p) j -> p tt j", p=128)
                        sm["ust"].inc(g.dma_start(
                            out=out_apx, in_=ustage[b][:].rearrange(
                                "p (tt j) -> p tt j", tt=NT)), 16)
                        if mode == "dequant":
                            VC.w(g, dq_m[oc])
                            sm["ust"].inc(g.dma_start(out=wcache[oc], in_=wocbf[b][:]), 16)

        def emit_collective():
            with nc.Block() as blk:
                @blk.gpsimd
                def _(g):
                    sm["evst"].w(g)
                    sm["cc"].inc(g.collective_compute(
                        "AllGather", OP.bypass, replica_groups=RG,
                        ins=[kvb_loc.opt()], outs=[kvb_gath.opt()]), 1)

        def emit_attb():
            KTQ = [20, 24, 28, 32]
            with ExitStack() as ls, nc.Block() as blk:
                L = ls.enter_context
                msk_sb = L(nc.sbuf_tensor("msk_sb", [128, NF * 512], BF16))
                k_sb = L(nc.sbuf_tensor("kb_sb", [128, 2 * T], BF16))
                v_sb = L(nc.sbuf_tensor("vb_sb", [128, 2 * T], BF16))
                vt_b = L(nc.sbuf_tensor("vt_b", [128, 32 * 130], BF16))
                q_sb = [L(nc.sbuf_tensor(f"qsb{i}", [128, 512], BF16)) for i in range(2)]
                p_sb = [L(nc.sbuf_tensor(f"psb{i}", [128, 1024], BF16)) for i in range(4)]
                rzb = L(nc.sbuf_tensor("rzb", [128, 1], F32))
                cstg = L(nc.sbuf_tensor("cstg", [128, 128], BF16))
                ctstg = [L(nc.sbuf_tensor(f"cts{i}", [128, 512], BF16)) for i in range(2)]
                ps_S = [L(nc.psum_tensor(f"psS{i}", [128, 1024], F32)) for i in range(2)]
                ps_C = [L(nc.psum_tensor(f"psC{i}", [128, 512], F32)) for i in range(2)]
                ps_V = L(nc.psum_tensor("psV", [128, 1024], BF16))
                ps_X = L(nc.psum_tensor("psX", [128, 512], BF16))

                mb, qb0, qkb, stb = sm["msk"].n, sm["aQ"].n, sm["qk"].n, sm["ust"].n
                peb, acb, vcb, trb = PEs.n, ACh.n, VC.n, TRs.n
                NU, NQ = 32, 128
                KT_of = [KTQ[nq // 32] for nq in range(NQ)]
                NR_of = [(KTQ[u // 8] + 7) // 8 for u in range(NU)]

                vtr_m = [[0] * 4 for _ in range(NU)]
                ctr_m = [[0] * 4 for _ in range(NQ)]
                cnt = trb
                for u_ in range(NU):
                    for r in range(NR_of[u_]):
                        cnt += 1; vtr_m[u_][r] = cnt
                    for g in range(4):
                        for i in range(4):
                            cnt += 1; ctr_m[u_ * 4 + g][i] = cnt
                S_m = [[0] * KT_of[nq] for nq in range(NQ)]
                PV_m = [[0] * KT_of[nq] for nq in range(NQ)]
                cnt = peb
                for nq in range(NQ):
                    P = KT_of[nq] // 2
                    for kt in range(4):
                        cnt += 1; S_m[nq][kt] = cnt
                    for p in range(P):
                        cnt += 1; PV_m[nq][2 * p] = cnt
                        cnt += 1; PV_m[nq][2 * p + 1] = cnt
                        if p < P - 2:
                            cnt += 1; S_m[nq][2 * p + 4] = cnt
                            cnt += 1; S_m[nq][2 * p + 5] = cnt
                exp_m = [[0] * (KT_of[nq] // 2) for nq in range(NQ)]
                cnt = acb
                for nq in range(NQ):
                    for p in range(KT_of[nq] // 2):
                        cnt += 1; exp_m[nq][p] = cnt
                vte_m = [[0] * 4 for _ in range(NU)]
                mskm = [[0] * (KT_of[u * 4] // 2) for u in range(NU) for _ in (0,)]
                mskm = [[0] * (KT_of[nq] // 2) for nq in range(NQ)]
                rec_m = [[0] * 4 for _ in range(NQ)]
                ts_m = [[0] * 4 for _ in range(NQ)]
                xe_m = [[0] * 4 for _ in range(NQ)]
                cnt = vcb + 1
                ones_m = cnt
                for u_ in range(NU):
                    for r in range(NR_of[u_]):
                        cnt += 1; vte_m[u_][r] = cnt
                    for g in range(4):
                        nq = u_ * 4 + g
                        for p in range(KT_of[nq] // 2):
                            cnt += 1; mskm[nq][p] = cnt
                        for i in range(4):
                            cnt += 1; rec_m[nq][i] = cnt
                            cnt += 1; ts_m[nq][i] = cnt
                            cnt += 1; xe_m[nq][i] = cnt

                def emit_S(t, nq, kt):
                    sb2 = (kt // 2) % 2
                    half = (kt % 2) * 512
                    if kt >= 4:
                        ACh.w(t, exp_m[nq][kt // 2 - 2])
                    elif nq >= 1:
                        pp = KT_of[nq - 1] // 2 - 2 + kt // 2
                        ACh.w(t, exp_m[nq - 1][pp])
                    PEs.inc(t.matmul(
                        ps_S[sb2][:, half:half + 512],
                        lhsT=k_sb[:, kt * 128:(kt + 1) * 128],
                        rhs=q_sb[nq % 2][:],
                        start=True, stop=True), 1, expect=S_m[nq][kt])

                def emit_PV(t, nq, kt):
                    KT = KT_of[nq]
                    sb4 = (kt // 2) % 4
                    half = (kt % 2) * 512
                    VC.w(t, mskm[nq][kt // 2])
                    if kt == 0 and nq >= 1:
                        VC.w(t, ts_m[nq - 1][3])
                    for i in range(4):
                        bk, off = i // 2, (i % 2) * 256
                        ins = t.matmul(
                            ps_C[bk][:, off:off + 129],
                            lhsT=p_sb[sb4][:, half + i * 128:half + (i + 1) * 128],
                            rhs=vt_b[:].rearrange(
                                "p (kk c) -> p kk c", kk=32)[:, kt, 0:129],
                            start=(kt == 0 and i % 2 == 0), stop=(kt == KT - 1))
                    PEs.inc(ins, 1, expect=PV_m[nq][kt])

                @blk.sync
                def _(sy):
                    sm["cc"].w(sy)
                    sm["evst"].w(sy)
                    nq = 0
                    for qr in range(4):
                        if qr >= 1:
                            VC.w(sy, mskm[qr * 32 - 1][-1])
                        src = maskT[:, qr * 512:(qr + 1) * 512].rearrange(
                            "(c p) t -> p c t", p=128)
                        sm["msk"].inc(sy.dma_start(out=msk_sb[:].rearrange(
                            "p (c t) -> p c t", c=NF), in_=src), 16)
                        for kvh in range(8):
                            u_ = qr * 8 + kvh
                            if u_ >= 1:
                                PEs.w(sy, PV_m[u_ * 4 - 1][-1])
                                TRs.w(sy, vtr_m[u_ - 1][NR_of[u_ - 1] - 1])
                            for half in range(2):
                                sm["qk"].inc(sy.dma_start(
                                    out=k_sb[:, half * T:(half + 1) * T],
                                    in_=kvb_gath[half * T + kvh * 128:
                                                 half * T + (kvh + 1) * 128, :]), 16)
                                sm["qk"].inc(sy.dma_start(
                                    out=v_sb[:, half * T:(half + 1) * T],
                                    in_=kvb_gath[half * T + 1024 + kvh * 128:
                                                 half * T + 1024 + (kvh + 1) * 128, :]), 16)
                            for g in range(4):
                                h = kvh * 4 + g
                                if nq >= 2:
                                    PEs.w(sy, PV_m[nq - 2][-1])
                                sm["aQ"].inc(sy.dma_start(
                                    out=q_sb[nq % 2][:],
                                    in_=qb_dram[h * 128:(h + 1) * 128,
                                                qr * 512:(qr + 1) * 512]), 16)
                                nq += 1

                @blk.tensor
                def _(t):
                    nq = 0
                    for qr in range(4):
                        for kvh in range(8):
                            u_ = qr * 8 + kvh
                            t.wait_ge(sm["qk"].h, qkb + 64 * (u_ + 1))
                            for r in range(NR_of[u_]):
                                if u_ + r >= 1:
                                    if r >= 1:
                                        VC.w(t, vte_m[u_][r - 1])
                                    else:
                                        VC.w(t, vte_m[u_ - 1][NR_of[u_ - 1] - 1])
                                for c8 in range(8):
                                    kt = r * 8 + c8
                                    ins = t.transpose(
                                        ps_V[:, c8 * 128:(c8 + 1) * 128],
                                        v_sb[:, kt * 128:(kt + 1) * 128], id_bf[:])
                                TRs.inc(ins, 1, expect=vtr_m[u_][r])
                            for g in range(4):
                                KT = KT_of[nq]
                                P = KT // 2
                                t.wait_ge(sm["aQ"].h, qb0 + 16 * (nq + 1))
                                VC.w(t, vte_m[u_][NR_of[u_] - 1])
                                VC.w(t, ones_m)
                                for kt in range(4):
                                    emit_S(t, nq, kt)
                                for p in range(P):
                                    emit_PV(t, nq, 2 * p)
                                    emit_PV(t, nq, 2 * p + 1)
                                    if p < P - 2:
                                        emit_S(t, nq, 2 * p + 4)
                                        emit_S(t, nq, 2 * p + 5)
                                for i in range(4):
                                    VC.w(t, ts_m[nq][i])
                                    if nq >= 1 and i == 0:
                                        VC.w(t, xe_m[nq - 1][3])
                                    ins = t.transpose(ps_X[:, i * 128:(i + 1) * 128],
                                                      cstg[:], id_bf[:])
                                    TRs.inc(ins, 1, expect=ctr_m[nq][i])
                                nq += 1

                @blk.scalar
                def _(s):
                    for nq in range(NQ):
                        for p in range(KT_of[nq] // 2):
                            sb2 = p % 2
                            PEs.w(s, S_m[nq][2 * p + 1])
                            if p >= 4:
                                PEs.w(s, PV_m[nq][2 * p - 7])
                            elif nq >= 1:
                                PEs.w(s, PV_m[nq - 1][-1])
                            ACh.inc(s.activation(p_sb[p % 4][:], ps_S[sb2][:],
                                                 AF.Exp, bias=c0b[:],
                                                 scale=ISQ), 1,
                                    expect=exp_m[nq][p])

                @blk.vector
                def _(v):
                    VC.inc(v.memset(vt_b[:].rearrange(
                        "p (kk c) -> p kk c", kk=32)[:, :, 128:129], 1.0), 1,
                        expect=ones_m)
                    nq = 0
                    for qr in range(4):
                        v.wait_ge(sm["msk"].h, mb + 16 * (qr + 1))
                        for kvh in range(8):
                            u_ = qr * 8 + kvh
                            if u_ >= 1:
                                PEs.w(v, PV_m[u_ * 4 - 1][-1])
                            for r in range(NR_of[u_]):
                                TRs.w(v, vtr_m[u_][r])
                                dst_v = vt_b[:].rearrange("p (kk c) -> p kk c", kk=32)[
                                    :, r * 8:(r + 1) * 8, 0:128]
                                VC.inc(v.tensor_copy(dst_v, cv(ps_V[:], c=8)), 1,
                                       expect=vte_m[u_][r])
                            for g in range(4):
                                for p in range(KT_of[nq] // 2):
                                    ACh.w(v, exp_m[nq][p])
                                    VC.inc(v.tensor_tensor(
                                        p_sb[p % 4][:].rearrange(
                                            "p (a t) -> p a t", a=2),
                                        p_sb[p % 4][:].rearrange(
                                            "p (a t) -> p a t", a=2),
                                        cv(msk_sb[:])[:, 2 * p:2 * p + 2, :],
                                        OP.mult), 1, expect=mskm[nq][p])
                                PEs.w(v, PV_m[nq][-1])
                                if nq >= 2:
                                    sm["ust"].w(v, stb + 16 * (nq - 1))
                                for i in range(4):
                                    bk, off = i // 2, (i % 2) * 256
                                    VC.inc(v.reciprocal(
                                        rzb[:], ps_C[bk][:, off + 128:off + 129]), 1,
                                        expect=rec_m[nq][i])
                                    VC.w(v, rec_m[nq][i])
                                    if i >= 1:
                                        TRs.w(v, ctr_m[nq][i - 1])
                                    VC.inc(v.tensor_scalar(
                                        cstg[:], ps_C[bk][:, off:off + 128],
                                        rzb[:], None, OP.mult), 1,
                                        expect=ts_m[nq][i])
                                    TRs.w(v, ctr_m[nq][i])
                                    VC.inc(v.tensor_copy(
                                        ctstg[nq % 2][:, i * 128:(i + 1) * 128],
                                        ps_X[:, i * 128:(i + 1) * 128]), 1,
                                        expect=xe_m[nq][i])
                                nq += 1

                @blk.gpsimd
                def _(g):
                    nq = 0
                    for qr in range(4):
                        for kvh in range(8):
                            for gg in range(4):
                                h = kvh * 4 + gg
                                VC.w(g, xe_m[nq][3])
                                sm["ust"].inc(g.dma_start(
                                    out=ctxbT[h * 128:(h + 1) * 128,
                                              qr * 512:(qr + 1) * 512],
                                    in_=ctstg[nq % 2][:]), 16)
                                nq += 1

        sub_ctx = ExitStack()
        xnT = sub_ctx.enter_context(nc.sbuf_tensor("xnT1", [128, NF * T], BF16))
        emit_boundary(1, xnT)

        def slab_jobs(slab_ap, bias_sb_):
            jobs = []
            for oc in range(NF):
                stores = [(slab_ap[sl][oc * 128:(oc + 1) * 128, :],
                           sl * 512, (sl + 1) * 512) for sl in range(NSLAB)]
                jobs.append(dict(col=oc,
                                 bias=(bias_sb_[:, oc:oc + 1]
                                       if bias_sb_ is not None else None),
                                 stores=stores))
            return jobs

        emit_proj(wq_t, slab_jobs(qa_slab, bq_sb), "q", xnT)
        emit_proj(wk_t, slab_jobs(ka_slab, bk_sb), "k", xnT)
        emit_proj(wv_t, slab_jobs(va_slab, bv_sb), "v", xnT)
        sub_ctx.close()
        emit_mix()
        if n_sub >= 2:
            sub_ctx = ExitStack()
            xnT = sub_ctx.enter_context(nc.sbuf_tensor("xnT2", [128, NF * T], BF16))
            emit_boundary(2, xnT)
            emit_fmout("dequant", "f1", xnT, bias_sb=fb_sb)
            sub_ctx.close()
        if n_sub >= 3:
            sub_ctx = ExitStack()
            xnT = sub_ctx.enter_context(nc.sbuf_tensor("xnT3", [128, NF * T], BF16))
            emit_boundary(3, xnT)
            kv_jobs = [dict(col=oc, bias=None,
                            stores=[(kvb_loc[(oc - 32) * 128:(oc - 31) * 128, :], 0, T)])
                       for oc in range(32, 48)]
            emit_proj(wqkv_t, kv_jobs, "kv", xnT)
            emit_collective()
            q_jobs = [dict(col=oc, bias=None,
                           stores=[(qb_dram[oc * 128:(oc + 1) * 128, :], 0, T)])
                      for oc in range(32)]
            emit_proj(wqkv_t, q_jobs, "qb", xnT)
            sub_ctx.close()
            if "attb" not in os.environ.get("KSKIP", ""):
                emit_attb()
            sub_ctx = ExitStack()
            xnT = sub_ctx.enter_context(nc.sbuf_tensor("ctxT", [128, NF * T], BF16))
            with nc.Block() as blk:
                @blk.sync
                def _(sy):
                    sm["ust"].w(sy)
                    src = ctxbT.rearrange("(c p) t -> p c t", p=128)
                    sm["qk"].inc(sy.dma_start(out=cv(xnT[:]), in_=src), 16)
                @blk.tensor
                def _(t):
                    sm["qk"].w(t)
            if "dense" not in os.environ.get("KSKIP", ""):
                emit_fmout("w32", "dn", xnT, w_ap=wd_t, bias_sb=None)
            sub_ctx.close()
        if n_sub >= 4:
            sub_ctx = ExitStack()
            xnT = sub_ctx.enter_context(nc.sbuf_tensor("xnT4", [128, NF * T], BF16))
            emit_boundary(4, xnT)
            emit_fmout("cache", "f2", xnT)
            sub_ctx.close()
        if dbg:
            with nc.Block() as blk:
                @blk.sync
                def _(sy):
                    sm["ust"].w(sy)
                    sm["cc"].w(sy)
                    sm["aux"].inc(sy.dma_start(out=dbg["d_gath"], in_=kvb_gath), 16)
                    sm["aux"].inc(sy.dma_start(out=dbg["d_ctxb"], in_=ctxbT), 16)
                    sm["aux"].inc(sy.dma_start(out=dbg["d_qb"], in_=qb_dram), 16)
                    sm["aux"].inc(sy.dma_start(out=dbg["d_u"], in_=u_dram), 16)
                    sm["aux"].w(sy)
        emit_boundary(n_sub + 1)

    return nc


def _prep_shared(inputs):
    def tr(w):
        return np.ascontiguousarray(w.astype(np.float32, copy=False).T)
    sh = {}
    sh["wq_t"] = tr(inputs["L0_attn_a_wq"])
    sh["wk_t"] = tr(inputs["L0_attn_a_wk"])
    sh["wv_t"] = tr(inputs["L0_attn_a_wv"])
    sh["bq_r"] = np.ascontiguousarray(
        inputs["L0_attn_a_bq"].astype(np.float32).reshape(NF, 128).T)
    sh["bk_r"] = np.ascontiguousarray(
        inputs["L0_attn_a_bk"].astype(np.float32).reshape(NF, 128).T)
    sh["bv_r"] = np.ascontiguousarray(
        inputs["L0_attn_a_bv"].astype(np.float32).reshape(NF, 128).T)
    sh["wqkv_t"] = tr(inputs["L0_attn_b_w_qkv"])
    sh["wd_t"] = tr(inputs["L0_attn_b_w_dense"])
    sh["qv_t"] = np.ascontiguousarray(
        inputs["L0_ffn_q_vals"].transpose(1, 2, 0).reshape(D, D)).astype(np.int32)
    sh["qs_t"] = np.ascontiguousarray(inputs["L0_ffn_q_scales"].astype(np.float32).T)
    sh["fb_r"] = np.ascontiguousarray(
        inputs["L0_ffn_bias"].astype(np.float32).reshape(NF, 128).T)
    tk = np.arange(2 * T)[:, None]
    sh["_masks"] = {}
    for half in range(2):
        tq = half * T + np.arange(T)[None, :]
        sh["_masks"][half] = np.ascontiguousarray((tk <= tq).astype(BFNP))
    return sh


_dbg_results = None


def kernel(n_sub=4, **inputs):
    _ = inputs.get("L0_attn_a_wl"), inputs.get("L0_attn_a_bl")
    sh = _prep_shared(inputs)
    in_maps = []
    for c in range(8):
        b, half = c // 2, c % 2
        m = dict(x0=np.ascontiguousarray(
            inputs["L0_attn_a_hidden_states"][b, half * T:(half + 1) * T, :]
        ).astype(np.float32, copy=False))
        for k2, v2 in sh.items():
            if k2 == "_masks":
                m["maskT"] = sh["_masks"][half]
            else:
                m[k2] = v2
        in_maps.append(m)
    nc = build(n_sub=n_sub)
    res = run_bass_kernel_spmd(nc, in_maps, core_ids=list(range(8)))
    out = np.empty((4, 4096, 4096), np.float32)
    for c in range(8):
        b, half = c // 2, c % 2
        out[b, half * T:(half + 1) * T, :] = res.results[c]["y"]
    if os.environ.get("KDBG"):
        global _dbg_results
        _dbg_results = res.results
    return out


# version 06 (final)
# speedup vs baseline: 1.0819x; 1.0819x over previous
import os
import numpy as np
import ml_dtypes
from contextlib import ExitStack

import concourse.bass as bass
import concourse.mybir as mybir
from concourse.alu_op_type import AluOpType as OP
from concourse.bass_utils import run_bass_kernel_spmd

F32 = mybir.dt.float32
BF16 = mybir.dt.bfloat16
I32 = mybir.dt.int32
AF = mybir.ActivationFunctionType
AX = mybir.AxisListType
BFNP = ml_dtypes.bfloat16

T = 2048
D = 4096
NT = T // 128
NF = D // 128
NSLAB, SLAB = 4, 512
C0 = 8.0
EPS = 1e-5
ISQ = float(1.0 / np.sqrt(128.0))
RG = [[0, 1], [2, 3], [4, 5], [6, 7]]


class Sem:
    def __init__(self, ctx, nc, name):
        self.h = ctx.enter_context(nc.semaphore(name))
        self.n = 0

    def inc(self, instr, k=1, expect=None):
        instr.then_inc(self.h, k)
        self.n += k
        if expect is not None:
            assert self.n == expect, f"sem mark drift: {self.n} != {expect}"
        return self.n

    def w(self, eng, val=None):
        eng.wait_ge(self.h, self.n if val is None else val)


def cv(apx, c=NF):
    return apx.rearrange("p (c t) -> p c t", c=c)


def build(n_sub=4):
    nc = bass.Bass("TRN2", target_bir_lowering=False)

    ap = lambda t: t.ap()
    x0 = ap(nc.dram_tensor("x0", [T, D], F32, kind="ExternalInput"))
    wq_t = ap(nc.dram_tensor("wq_t", [D, D], F32, kind="ExternalInput"))
    wk_t = ap(nc.dram_tensor("wk_t", [D, D], F32, kind="ExternalInput"))
    wv_t = ap(nc.dram_tensor("wv_t", [D, D], F32, kind="ExternalInput"))
    bq_r = ap(nc.dram_tensor("bq_r", [128, NF], F32, kind="ExternalInput"))
    bk_r = ap(nc.dram_tensor("bk_r", [128, NF], F32, kind="ExternalInput"))
    bv_r = ap(nc.dram_tensor("bv_r", [128, NF], F32, kind="ExternalInput"))
    wqkv_t = ap(nc.dram_tensor("wqkv_t", [D, 6144], F32, kind="ExternalInput"))
    wd_t = ap(nc.dram_tensor("wd_t", [D, D], F32, kind="ExternalInput"))
    qv_t = ap(nc.dram_tensor("qv_t", [D, D], I32, kind="ExternalInput"))
    qs_t = ap(nc.dram_tensor("qs_t", [128, D], F32, kind="ExternalInput"))
    fb_r = ap(nc.dram_tensor("fb_r", [128, NF], F32, kind="ExternalInput"))
    maskT = ap(nc.dram_tensor("maskT", [2 * T, T], BF16, kind="ExternalInput"))
    y = ap(nc.dram_tensor("y", [T, D], F32, kind="ExternalOutput"))

    xs0 = ap(nc.dram_tensor("xs0", [T, D], F32))
    xs1 = ap(nc.dram_tensor("xs1", [T, D], F32))
    u_dram = ap(nc.dram_tensor("u_dram", [T, D], F32))
    qa_slab = ap(nc.dram_tensor("qa_slab", [NSLAB, D, SLAB], BF16))
    ka_slab = ap(nc.dram_tensor("ka_slab", [NSLAB, D, SLAB], BF16))
    va_slab = ap(nc.dram_tensor("va_slab", [NSLAB, D, SLAB], BF16))
    qb_dram = ap(nc.dram_tensor("qb_dram", [D, T], BF16))
    kvb_loc = ap(nc.dram_tensor("kvb_loc", [T, T], BF16))
    kvb_gath = ap(nc.dram_tensor("kvb_gath", [2 * T, T], BF16))
    ctxbT = ap(nc.dram_tensor("ctxbT", [D, T], BF16))
    wcache = ap(nc.dram_tensor("wcache", [NF, 128, D], BF16))
    dbg = {}
    if os.environ.get("KDBG"):
        dbg["d_gath"] = ap(nc.dram_tensor("d_gath", [2 * T, T], BF16, kind="ExternalOutput"))
        dbg["d_ctxb"] = ap(nc.dram_tensor("d_ctxb", [D, T], BF16, kind="ExternalOutput"))
        dbg["d_qb"] = ap(nc.dram_tensor("d_qb", [D, T], BF16, kind="ExternalOutput"))
        dbg["d_u"] = ap(nc.dram_tensor("d_u", [T, D], F32, kind="ExternalOutput"))

    est = ExitStack()
    with est:
        E = est.enter_context
        sm = {k: Sem(est, nc, "s_" + k) for k in [
            "VC", "AC", "PC", "PE", "TR",
            "x", "u", "w0", "qk", "msk", "aQ",
            "xst", "ust", "evst", "cc", "aux",
        ]}
        VC, ACh, PEs, TRs = sm["VC"], sm["AC"], sm["PE"], sm["TR"]

        id_bf = E(nc.sbuf_tensor("id_bf", [128, 128], BF16))
        id_f32 = E(nc.sbuf_tensor("id_f32", [128, 128], F32))
        idv = E(nc.sbuf_tensor("idv", [128, 128], I32))
        bq_sb = E(nc.sbuf_tensor("bq_sb", [128, NF], F32))
        bk_sb = E(nc.sbuf_tensor("bk_sb", [128, NF], F32))
        fb_sb = E(nc.sbuf_tensor("fb_sb", [128, NF], F32))
        bv_sb = E(nc.sbuf_tensor("bv_sb", [128, NF], F32))
        c0b = E(nc.sbuf_tensor("c0b", [128, 1], F32))
        zrb = E(nc.sbuf_tensor("zrb", [128, 1], F32))
        epb = E(nc.sbuf_tensor("epb", [128, 1], F32))

        with nc.Block() as blk:
            @blk.sync
            def _(sy):
                sm["aux"].inc(sy.dma_start(out=bq_sb[:], in_=bq_r), 16)
                sm["aux"].inc(sy.dma_start(out=bk_sb[:], in_=bk_r), 16)
                sm["aux"].inc(sy.dma_start(out=fb_sb[:], in_=fb_r), 16)
                sm["aux"].inc(sy.dma_start(out=bv_sb[:], in_=bv_r), 16)

            @blk.gpsimd
            def _(g):
                sm["PC"].inc(g.iota(idv[:], pattern=[[-1, 128]], base=0,
                                    channel_multiplier=1), 1)

            @blk.vector
            def _(v):
                sm["PC"].w(v)
                sm["aux"].w(v)
                VC.inc(v.tensor_scalar(id_bf[:], idv[:], 0.0, None, OP.is_equal), 1)
                VC.inc(v.tensor_scalar(id_f32[:], idv[:], 0.0, None, OP.is_equal), 1)
                VC.inc(v.memset(c0b[:], -C0), 1)
                VC.inc(v.memset(zrb[:], 0.0), 1)
                VC.inc(v.memset(epb[:], EPS), 1)

        xsrc = {1: x0, 2: x0, 3: xs0, 4: xs1, 5: xs0}
        xdst = {2: xs0, 3: xs1, 4: xs0}

        def emit_boundary(k, xnT=None):
            last = (k == n_sub + 1)
            src = xsrc[k]
            dst = y if last else xdst.get(k)
            with ExitStack() as ls, nc.Block() as blk:
                L = ls.enter_context
                z_sb = [L(nc.sbuf_tensor(f"z{k}_{i}", [128, D], F32)) for i in range(2)]
                u_sb = [L(nc.sbuf_tensor(f"ubf{k}", [128, D], F32))]
                xn_bf = [L(nc.sbuf_tensor(f"xnb{k}_{i}", [128, D], BF16)) for i in range(2)]
                bnst = L(nc.sbuf_tensor(f"bnst{k}", [128, 48], F32))
                mv = L(nc.sbuf_tensor(f"mv{k}", [128, 2], F32))
                lnv = L(nc.sbuf_tensor(f"lnv{k}", [128, 1], F32))
                rstd = L(nc.sbuf_tensor(f"rstd{k}", [128, 1], F32))
                negmr = L(nc.sbuf_tensor(f"negmr{k}", [128, 1], F32))
                ps_tr = [L(nc.psum_tensor(f"pstr{k}_{i}", [128, 1024], BF16)) for i in range(2)]

                ust0, xst0 = sm["ust"].n, sm["xst"].n
                xb, ub0, stb = sm["x"].n, sm["u"].n, sm["xst"].n
                vcb, acb, trb = VC.n, ACh.n, TRs.n

                if not last:
                    def vstart(tt):
                        return vcb if tt == 0 else vcb + 12 + 16 * (tt - 1)
                    ev_m = [[0] * 4 for _ in range(NT)]
                    zadd_m = [0] * NT
                    aggr_m = [0] * NT
                    xn_m = [0] * NT
                    for tt in range(NT):
                        s0_ = vstart(tt)
                        o = 0
                        if tt >= 1:
                            for r in range(4):
                                ev_m[tt - 1][r] = s0_ + o + 1
                                o += 1
                        zadd_m[tt] = s0_ + o + 1; o += 1
                        o += 8
                        aggr_m[tt] = s0_ + o + 1; o += 1
                        o += 1
                        xn_m[tt] = s0_ + o + 1; o += 1
                    fs = vstart(NT)
                    for r in range(4):
                        ev_m[NT - 1][r] = fs + r + 1
                    rstd_m = [acb + 2 * (tt + 1) for tt in range(NT)]
                    tr_m = [[trb + 4 * tt + r + 1 for r in range(4)] for tt in range(NT)]
                else:
                    zadd_m = [vcb + tt + 1 for tt in range(NT)]
                    ev_m = xn_m = aggr_m = rstd_m = tr_m = None

                @blk.sync
                def _(sy):
                    sm["ust"].w(sy, ust0)
                    sm["xst"].w(sy, xst0)
                    for tt in range(NT):
                        b = tt % 2
                        if tt >= 2:
                            VC.w(sy, (zadd_m if last else xn_m)[tt - 2])
                            if dst is not None:
                                sy.wait_ge(sm["xst"].h, stb + 16 * (tt - 1))
                        sm["x"].inc(sy.dma_start(out=z_sb[b][:],
                                                 in_=src[tt * 128:(tt + 1) * 128, :]), 16)
                        if k >= 2:
                            if tt >= 1:
                                VC.w(sy, zadd_m[tt - 1])
                            sm["u"].inc(sy.dma_start(out=u_sb[0][:],
                                                     in_=u_dram[tt * 128:(tt + 1) * 128, :]), 16)

                @blk.vector
                def _(v):
                    for tt in range(NT):
                        b = tt % 2
                        if not last and tt >= 1:
                            for r in range(4):
                                TRs.w(v, tr_m[tt - 1][r])
                                dst_v = cv(xnT[:])[:, r * 8:(r + 1) * 8,
                                                   (tt - 1) * 128:tt * 128]
                                VC.inc(v.tensor_copy(dst_v, cv(ps_tr[r % 2][:], c=8)),
                                       1, expect=ev_m[tt - 1][r])
                        v.wait_ge(sm["x"].h, xb + 16 * (tt + 1))
                        if k >= 2:
                            v.wait_ge(sm["u"].h, ub0 + 16 * (tt + 1))
                            VC.inc(v.tensor_tensor(z_sb[b][:], z_sb[b][:],
                                                   u_sb[0][:], OP.add), 1,
                                   expect=zadd_m[tt])
                        else:
                            VC.inc(v.nop(), 1, expect=zadd_m[tt])
                        if not last:
                            for i in range(8):
                                VC.inc(v.bn_stats(bnst[:, 6 * i:6 * i + 6],
                                                  z_sb[b][:, 512 * i:512 * (i + 1)]), 1)
                            VC.w(v)
                            VC.inc(v.bn_aggr(mv[:], bnst[:]), 1, expect=aggr_m[tt])
                            ACh.w(v, rstd_m[tt])
                            nm = VC.inc(v.tensor_scalar(negmr[:], mv[:, 0:1], rstd[:],
                                                        -1.0, OP.mult, OP.mult), 1)
                            VC.w(v, nm)
                            VC.inc(v.tensor_scalar(xn_bf[b][:], z_sb[b][:], rstd[:],
                                                   negmr[:], OP.mult, OP.add), 1,
                                   expect=xn_m[tt])
                    if not last:
                        for r in range(4):
                            TRs.w(v, tr_m[NT - 1][r])
                            dst_v = cv(xnT[:])[:, r * 8:(r + 1) * 8,
                                               (NT - 1) * 128:NT * 128]
                            VC.inc(v.tensor_copy(dst_v, cv(ps_tr[r % 2][:], c=8)),
                                   1, expect=ev_m[NT - 1][r])

                @blk.scalar
                def _(s):
                    if last:
                        return
                    for tt in range(NT):
                        VC.w(s, aggr_m[tt])
                        c1 = ACh.inc(s.activation(lnv[:], mv[:, 1:2], AF.Ln,
                                                  bias=epb[:], scale=1.0), 1)
                        ACh.w(s, c1)
                        ACh.inc(s.activation(rstd[:], lnv[:], AF.Exp,
                                             bias=zrb[:], scale=-0.5), 1,
                                expect=rstd_m[tt])

                @blk.tensor
                def _(t):
                    if last:
                        return
                    for tt in range(NT):
                        b = tt % 2
                        VC.w(t, xn_m[tt])
                        for r in range(4):
                            pb = r % 2
                            if tt * 4 + r >= 2:
                                pr_tt, pr_r = divmod(tt * 4 + r - 2, 4)
                                VC.w(t, ev_m[pr_tt][pr_r])
                            for c8 in range(8):
                                c = r * 8 + c8
                                ins = t.transpose(ps_tr[pb][:, c8 * 128:(c8 + 1) * 128],
                                                  xn_bf[b][:, c * 128:(c + 1) * 128],
                                                  id_bf[:])
                            TRs.inc(ins, 1, expect=tr_m[tt][r])

                @blk.gpsimd
                def _(g):
                    if dst is None:
                        return
                    for tt in range(NT):
                        VC.w(g, zadd_m[tt])
                        sm["xst"].inc(g.dma_start(out=dst[tt * 128:(tt + 1) * 128, :],
                                                  in_=z_sb[tt % 2][:]), 16)
                    if last:
                        sm["xst"].w(g)

        def emit_proj(w_ap, jobs, tag, xnT=None):
            n = len(jobs)
            nd_prefix = [0]
            for j in jobs:
                nd_prefix.append(nd_prefix[-1] + len(j["stores"]))
            with ExitStack() as ls, nc.Block() as blk:
                L = ls.enter_context
                wstage = [L(nc.sbuf_tensor(f"wst{tag}{i}", [128, D], F32)) for i in range(2)]
                wbf = [L(nc.sbuf_tensor(f"wbf{tag}{i}", [128, D], BF16)) for i in range(2)]
                pstage = [L(nc.sbuf_tensor(f"pst{tag}{i}", [128, T], BF16)) for i in range(2)]
                psA = [L(nc.psum_tensor(f"psA{tag}{i}", [128, 2048], F32)) for i in range(2)]

                w0b, peb, acb, stb, vcb = (sm["w0"].n, PEs.n, ACh.n, sm["evst"].n, VC.n)
                cast_m = [vcb + i + 1 for i in range(n)]

                @blk.sync
                def _(sy):
                    for i in range(n):
                        b = i % 2
                        if i >= 2:
                            VC.w(sy, cast_m[i - 2])
                        oc = jobs[i]["col"]
                        src = w_ap[:, oc * 128:(oc + 1) * 128].rearrange(
                            "(c p) j -> p c j", p=128)
                        sm["w0"].inc(sy.dma_start(out=cv(wstage[b][:]), in_=src), 16)

                @blk.vector
                def _(v):
                    for i in range(n):
                        b = i % 2
                        v.wait_ge(sm["w0"].h, w0b + 16 * (i + 1))
                        if i >= 2:
                            PEs.w(v, peb + i - 1)
                        VC.inc(v.tensor_copy(wbf[b][:], wstage[b][:]), 1,
                               expect=cast_m[i])

                @blk.tensor
                def _(t):
                    for i in range(n):
                        b = i % 2
                        VC.w(t, cast_m[i])
                        if i >= 2:
                            ACh.w(t, acb + 4 * (i - 1))
                        for ts in range(4):
                            for fc in range(NF):
                                ins = t.matmul(psA[b][:, ts * 512:(ts + 1) * 512],
                                               lhsT=wbf[b][:, fc * 128:(fc + 1) * 128],
                                               rhs=cv(xnT[:])[:, fc, ts * 512:(ts + 1) * 512],
                                               start=(fc == 0), stop=(fc == NF - 1))
                        PEs.inc(ins, 1, expect=peb + i + 1)

                @blk.scalar
                def _(s):
                    for i, j in enumerate(jobs):
                        b = i % 2
                        PEs.w(s, peb + i + 1)
                        if i >= 2:
                            s.wait_ge(sm["evst"].h, stb + 16 * nd_prefix[i - 1])
                        for ts in range(4):
                            ACh.inc(s.activation(
                                pstage[b][:, ts * 512:(ts + 1) * 512],
                                psA[b][:, ts * 512:(ts + 1) * 512], AF.Identity,
                                bias=(j["bias"] if j["bias"] is not None else zrb[:]),
                                scale=1.0), 1)

                @blk.gpsimd
                def _(g):
                    for i, j in enumerate(jobs):
                        b = i % 2
                        ACh.w(g, acb + 4 * (i + 1))
                        for (out_ap, lo, hi) in j["stores"]:
                            sm["evst"].inc(g.dma_start(out=out_ap,
                                                       in_=pstage[b][:, lo:hi]), 16)

        def emit_mix():
            with ExitStack() as ls, nc.Block() as blk:
                L = ls.enter_context
                qs_sb = L(nc.sbuf_tensor("qs_sb", [128, NF * SLAB], BF16))
                ks_sb = L(nc.sbuf_tensor("ks_sb", [128, NF * SLAB], BF16))
                vs_sb = L(nc.sbuf_tensor("vs_sb", [128, NF * SLAB], BF16))
                p4 = L(nc.sbuf_tensor("p4", [128, 256], BF16))
                p4t = L(nc.sbuf_tensor("p4t", [128, 256], BF16))
                zt = L(nc.sbuf_tensor("zt", [128, 8], F32))
                rz = L(nc.sbuf_tensor("rz", [128, 8], F32))
                vt_sb = L(nc.sbuf_tensor("vt_sb", [128, 1024], BF16))
                ctx_acc = [L(nc.sbuf_tensor(f"ctxacc{i}", [128, D], F32)) for i in range(2)]
                ps_s = L(nc.psum_tensor("ps_s", [128, 256], F32))
                ps_v = L(nc.psum_tensor("ps_v", [128, 1024], BF16))
                ps_c = [L(nc.psum_tensor(f"ps_c{i}", [128, 512], F32)) for i in range(2)]

                qkb, ustb, evstb = sm["qk"].n, sm["ust"].n, sm["evst"].n
                NSG = SLAB // 32
                NIT = NSLAB * NSG
                peb, acb, vcb = PEs.n, ACh.n, VC.n

                A_m = [peb + 3 * it + 1 for it in range(NIT)]
                C_m = [peb + 3 * it + 2 for it in range(NIT)]
                D_m = [peb + 3 * it + 3 for it in range(NIT)]
                exp_m = [acb + it + 1 for it in range(NIT)]
                red_m = [0] * NIT; rec_m = [0] * NIT
                ptr_m = [0] * NIT; vte_m = [0] * NIT
                ctx_m = [[0] * 8 for _ in range(NIT)]
                cnt = vcb
                for it in range(NIT):
                    if it >= 1:
                        for g in range(8):
                            cnt += 1; ctx_m[it - 1][g] = cnt
                    cnt += 1; red_m[it] = cnt
                    cnt += 1; rec_m[it] = cnt
                    cnt += 1; ptr_m[it] = cnt
                    cnt += 1; vte_m[it] = cnt
                for g in range(8):
                    cnt += 1; ctx_m[NIT - 1][g] = cnt

                @blk.sync
                def _(sy):
                    sm["evst"].w(sy, evstb)
                    for sl in range(NSLAB):
                        if sl >= 1:
                            PEs.w(sy, C_m[sl * NSG - 1])
                        for dst_sb, slab in ((qs_sb, qa_slab), (ks_sb, ka_slab),
                                             (vs_sb, va_slab)):
                            src = slab[sl].rearrange("(c p) t -> p c t", p=128)
                            sm["qk"].inc(sy.dma_start(out=cv(dst_sb[:]), in_=src), 16)

                @blk.tensor
                def _(t):
                    for sl in range(NSLAB):
                        t.wait_ge(sm["qk"].h, qkb + 48 * (sl + 1))
                        qv_ = cv(qs_sb[:], c=NF).rearrange("p c t -> p t c")
                        kv_ = cv(ks_sb[:], c=NF).rearrange("p c t -> p t c")
                        vv_ = cv(vs_sb[:], c=NF).rearrange("p c t -> p t c")
                        for sg in range(NSG):
                            it = sl * NSG + sg
                            t0 = sg * 32
                            if it >= 1:
                                ACh.w(t, exp_m[it - 1])
                            for g in range(8):
                                for i in range(4):
                                    tk = t0 + g * 4 + i
                                    ins = t.matmul(ps_s[32 * i:32 * i + 32,
                                                        g * 32:(g + 1) * 32],
                                                   lhsT=qv_[:, tk, :], rhs=kv_[:, tk, :],
                                                   start=True, stop=True,
                                                   tile_position=(0, 32 * i))
                            PEs.inc(ins, 1, expect=A_m[it])
                            if it >= 1:
                                VC.w(t, vte_m[it - 1])
                            for g in range(8):
                                for i in range(4):
                                    tk = t0 + g * 4 + i
                                    ins = t.transpose(ps_v[32 * i:32 * i + 32,
                                                           g * 128:(g + 1) * 128],
                                                      vv_[:, tk, :], id_bf[:],
                                                      tile_position=(0, 32 * i))
                            PEs.inc(ins, 1, expect=C_m[it])
                            VC.w(t, ptr_m[it])
                            VC.w(t, vte_m[it])
                            if it >= 1:
                                VC.w(t, ctx_m[it - 1][7])
                            for g in range(8):
                                bk = g // 4
                                off = (g % 4) * 128
                                for i in range(4):
                                    ins = t.matmul(ps_c[bk][32 * i:32 * i + 32,
                                                           off:off + 128],
                                                   lhsT=p4t[32 * i:32 * i + 32,
                                                            g * 32:(g + 1) * 32],
                                                   rhs=vt_sb[32 * i:32 * i + 32,
                                                             g * 128:(g + 1) * 128],
                                                   start=True, stop=True,
                                                   tile_position=(32 * i, 32 * i))
                            PEs.inc(ins, 1, expect=D_m[it])

                @blk.scalar
                def _(s):
                    for it in range(NIT):
                        PEs.w(s, A_m[it])
                        if it >= 1:
                            VC.w(s, ptr_m[it - 1])
                        ACh.inc(s.activation(p4[:], ps_s[:], AF.Exp,
                                             bias=c0b[:], scale=ISQ), 1,
                                expect=exp_m[it])

                def emit_ctx_evac(v, it):
                    PEs.w(v, D_m[it])
                    VC.w(v, rec_m[it])
                    ca = ctx_acc[(it // 4) % 2]
                    if it % 4 == 0 and it >= 8:
                        sm["ust"].w(v, ustb + 16 * (it // 4 - 1))
                    for g in range(8):
                        bk = g // 4
                        off = (g % 4) * 128
                        dsl = ((it % 4) * 8 + g) * 128
                        VC.inc(v.tensor_scalar(
                            ca[:, dsl:dsl + 128], ps_c[bk][:, off:off + 128],
                            rz[:, g:g + 1], None, OP.mult), 1,
                            expect=ctx_m[it][g])

                @blk.vector
                def _(v):
                    for it in range(NIT):
                        if it >= 1:
                            emit_ctx_evac(v, it - 1)
                        ACh.w(v, exp_m[it])
                        VC.inc(v.tensor_reduce(zt[:], cv(p4[:], c=8), AX.X, OP.add),
                               1, expect=red_m[it])
                        VC.w(v, red_m[it])
                        VC.inc(v.reciprocal(rz[:], zt[:]), 1, expect=rec_m[it])
                        if it >= 1:
                            PEs.w(v, D_m[it - 1])
                        VC.inc(v.transpose(p4t[:], p4[:]), 1, expect=ptr_m[it])
                        PEs.w(v, C_m[it])
                        VC.inc(v.tensor_copy(vt_sb[:], ps_v[:]), 1, expect=vte_m[it])
                    emit_ctx_evac(v, NIT - 1)

                @blk.gpsimd
                def _(g_):
                    for w in range(NIT // 4):
                        VC.w(g_, ctx_m[4 * w + 3][7])
                        ca = ctx_acc[w % 2]
                        out_apx = u_dram[w * 128:(w + 1) * 128, :].rearrange(
                            "(s g i) (h d) -> (i h) (s g) d", s=4, g=8, i=4, h=32)
                        in_apx = ca[:].rearrange("p (b d) -> p b d", b=32)
                        sm["ust"].inc(g_.dma_start(out=out_apx, in_=in_apx), 16)


        def emit_fmout(mode, tag, xnT=None, w_ap=None, bias_sb=None):
            with ExitStack() as ls, nc.Block() as blk:
                L = ls.enter_context
                wocbf = [L(nc.sbuf_tensor(f"woc{tag}{i}", [128, D], BF16)) for i in range(2)]
                if mode == "dequant":
                    qvst = [L(nc.sbuf_tensor(f"qvs{tag}0", [128, D], I32))]
                    sclb = L(nc.sbuf_tensor(f"scl{tag}", [128, D], F32))
                if mode == "w32":
                    wstage = [L(nc.sbuf_tensor(f"wsg{tag}{i}", [128, D], F32)) for i in range(2)]
                ystage = [L(nc.sbuf_tensor(f"yst{tag}0", [128, T], F32))]
                ustage = [L(nc.sbuf_tensor(f"usg{tag}{i}", [128, T], F32)) for i in range(2)]
                psA = [L(nc.psum_tensor(f"psF{tag}{i}", [128, 2048], F32)) for i in range(2)]

                w0b, peb, acb, trb, ustb = (sm["w0"].n, PEs.n, ACh.n, TRs.n, sm["ust"].n)
                vcb = VC.n
                per = {"dequant": 5, "cache": 4, "w32": 5}[mode]
                dq_m = [0] * NF
                uev_m = [[0] * 4 for _ in range(NF)]
                for oc in range(NF):
                    base = vcb + per * oc
                    o = 1 if mode != "cache" else 0
                    if mode != "cache":
                        dq_m[oc] = base + 1
                    for r in range(4):
                        uev_m[oc][r] = base + o + r + 1
                ns_per = 2 if mode == "dequant" else 1
                nload = {"dequant": 80, "cache": 16, "w32": 16}[mode]

                @blk.sync
                def _(sy):
                    for oc in range(NF):
                        b = oc % 2
                        if mode == "dequant":
                            if oc >= 1:
                                VC.w(sy, dq_m[oc - 1])
                            src = qv_t[:, oc * 128:(oc + 1) * 128].rearrange(
                                "(c p) j -> p c j", p=128)
                            sm["w0"].inc(sy.dma_start(out=cv(qvst[0][:]), in_=src), 16)
                            if oc >= 1:
                                VC.w(sy, dq_m[oc - 1])
                            qsv = qs_t.rearrange("(c g) j -> g c j", g=4)
                            for g in range(4):
                                src2 = qsv[g, :, oc * 128:(oc + 1) * 128
                                           ].partition_broadcast(32)
                                dst2 = cv(sclb[32 * g:32 * g + 32, :])
                                sm["w0"].inc(sy.dma_start(out=dst2, in_=src2), 16)
                        elif mode == "cache":
                            if oc >= 2:
                                PEs.w(sy, peb + 2 * (oc - 1))
                            sm["w0"].inc(sy.dma_start(out=wocbf[b][:], in_=wcache[oc]), 16)
                        else:
                            if oc >= 2:
                                VC.w(sy, dq_m[oc - 2])
                            src = w_ap[:, oc * 128:(oc + 1) * 128].rearrange(
                                "(c p) j -> p c j", p=128)
                            sm["w0"].inc(sy.dma_start(out=cv(wstage[b][:]), in_=src), 16)

                @blk.vector
                def _(v):
                    for oc in range(NF):
                        b = oc % 2
                        if mode == "dequant":
                            v.wait_ge(sm["w0"].h, w0b + 80 * (oc + 1))
                            if oc >= 2:
                                PEs.w(v, peb + 2 * (oc - 1))
                            VC.inc(v.scalar_tensor_tensor(
                                wocbf[b][:], qvst[0][:], -128.0, sclb[:],
                                OP.add, OP.mult), 1, expect=dq_m[oc])
                        elif mode == "w32":
                            v.wait_ge(sm["w0"].h, w0b + 16 * (oc + 1))
                            if oc >= 2:
                                PEs.w(v, peb + 2 * (oc - 1))
                            VC.inc(v.tensor_copy(wocbf[b][:], wstage[b][:]), 1,
                                   expect=dq_m[oc])
                        if oc >= 2:
                            sm["ust"].w(v, ustb + 16 * ns_per * (oc - 1))
                        for r in range(4):
                            TRs.w(v, trb + oc * 4 + r + 1)
                            VC.inc(v.tensor_copy(
                                ustage[b][:, r * 512:(r + 1) * 512],
                                psA[b][:, r * 512:(r + 1) * 512]), 1,
                                expect=uev_m[oc][r])

                @blk.tensor
                def _(t):
                    for oc in range(NF):
                        b = oc % 2
                        if mode == "cache":
                            t.wait_ge(sm["w0"].h, w0b + 16 * (oc + 1))
                        else:
                            VC.w(t, dq_m[oc])
                        if oc >= 2:
                            VC.w(t, uev_m[oc - 2][3])
                        for ts in range(4):
                            for fc in range(NF):
                                ins = t.matmul(psA[b][:, ts * 512:(ts + 1) * 512],
                                               lhsT=wocbf[b][:, fc * 128:(fc + 1) * 128],
                                               rhs=cv(xnT[:])[:, fc,
                                                              ts * 512:(ts + 1) * 512],
                                               start=(fc == 0), stop=(fc == NF - 1))
                        PEs.inc(ins, 1, expect=peb + 2 * oc + 1)
                        ACh.w(t, acb + 4 * (oc + 1))
                        for tt in range(NT):
                            r, qq = divmod(tt, 4)
                            ins = t.transpose(psA[b][:, r * 512 + qq * 128:
                                                     r * 512 + (qq + 1) * 128],
                                              ystage[0][:, tt * 128:(tt + 1) * 128],
                                              id_f32[:])
                            if qq == 3:
                                TRs.inc(ins, 1, expect=trb + oc * 4 + r + 1)
                        PEs.inc(t.nop(), 1, expect=peb + 2 * oc + 2)

                @blk.scalar
                def _(s):
                    for oc in range(NF):
                        b = oc % 2
                        PEs.w(s, peb + 2 * oc + 1)
                        if oc >= 1:
                            TRs.w(s, trb + 4 * oc)
                        for ts in range(4):
                            ACh.inc(s.activation(
                                ystage[0][:, ts * 512:(ts + 1) * 512],
                                psA[b][:, ts * 512:(ts + 1) * 512], AF.Identity,
                                bias=(bias_sb[:, oc:oc + 1] if bias_sb is not None
                                      else zrb[:]), scale=1.0), 1)

                @blk.gpsimd
                def _(g):
                    for oc in range(NF):
                        b = oc % 2
                        VC.w(g, uev_m[oc][3])
                        out_apx = u_dram[:, oc * 128:(oc + 1) * 128].rearrange(
                            "(tt p) j -> p tt j", p=128)
                        sm["ust"].inc(g.dma_start(
                            out=out_apx, in_=ustage[b][:].rearrange(
                                "p (tt j) -> p tt j", tt=NT)), 16)
                        if mode == "dequant":
                            VC.w(g, dq_m[oc])
                            sm["ust"].inc(g.dma_start(out=wcache[oc], in_=wocbf[b][:]), 16)

        def emit_collective():
            with nc.Block() as blk:
                @blk.gpsimd
                def _(g):
                    sm["evst"].w(g)
                    sm["cc"].inc(g.collective_compute(
                        "AllGather", OP.bypass, replica_groups=RG,
                        ins=[kvb_loc.opt()], outs=[kvb_gath.opt()]), 1)

        def emit_attb():
            KTQ = [20, 24, 28, 32]
            with ExitStack() as ls, nc.Block() as blk:
                L = ls.enter_context
                msk_sb = L(nc.sbuf_tensor("msk_sb", [128, NF * 512], BF16))
                k_sb = L(nc.sbuf_tensor("kb_sb", [128, 2 * T], BF16))
                v_sb = L(nc.sbuf_tensor("vb_sb", [128, 2 * T], BF16))
                vt_b = L(nc.sbuf_tensor("vt_b", [128, 32 * 130], BF16))
                q_sb = [L(nc.sbuf_tensor(f"qsb{i}", [128, 512], BF16)) for i in range(2)]
                p_sb = [L(nc.sbuf_tensor(f"psb{i}", [128, 1024], BF16)) for i in range(4)]
                rzb = L(nc.sbuf_tensor("rzb", [128, 1], F32))
                cstg = L(nc.sbuf_tensor("cstg", [128, 512], BF16))
                ctstg = [L(nc.sbuf_tensor(f"cts{i}", [128, 512], BF16)) for i in range(2)]
                ps_S = [L(nc.psum_tensor(f"psS{i}", [128, 1024], F32)) for i in range(2)]
                ps_C = [L(nc.psum_tensor(f"psC{i}", [128, 512], F32)) for i in range(2)]
                ps_V = L(nc.psum_tensor("psV", [128, 1024], BF16))
                ps_X = L(nc.psum_tensor("psX", [128, 512], BF16))

                mb, qb0, qkb, stb = sm["msk"].n, sm["aQ"].n, sm["qk"].n, sm["ust"].n
                peb, acb, vcb, trb = PEs.n, ACh.n, VC.n, TRs.n
                NU, NQ = 32, 128
                KT_of = [KTQ[nq // 32] for nq in range(NQ)]
                NR_of = [(KTQ[u // 8] + 7) // 8 for u in range(NU)]

                vtr_m = [[0] * 4 for _ in range(NU)]
                ctr_m = [[0] * 4 for _ in range(NQ)]
                cnt = trb
                for u_ in range(NU):
                    for r in range(NR_of[u_]):
                        cnt += 1; vtr_m[u_][r] = cnt
                    for g in range(4):
                        for i in range(4):
                            cnt += 1; ctr_m[u_ * 4 + g][i] = cnt
                S_m = [[0] * KT_of[nq] for nq in range(NQ)]
                PV_m = [[0] * KT_of[nq] for nq in range(NQ)]
                cnt = peb
                for nq in range(NQ):
                    P = KT_of[nq] // 2
                    for kt in range(4):
                        cnt += 1; S_m[nq][kt] = cnt
                    for p in range(P):
                        cnt += 1; PV_m[nq][2 * p] = cnt
                        cnt += 1; PV_m[nq][2 * p + 1] = cnt
                        if p < P - 2:
                            cnt += 1; S_m[nq][2 * p + 4] = cnt
                            cnt += 1; S_m[nq][2 * p + 5] = cnt
                exp_m = [[0] * (KT_of[nq] // 2) for nq in range(NQ)]
                cnt = acb
                for nq in range(NQ):
                    for p in range(KT_of[nq] // 2):
                        cnt += 1; exp_m[nq][p] = cnt
                vte_m = [[0] * 4 for _ in range(NU)]
                mskm = [[0] * (KT_of[u * 4] // 2) for u in range(NU) for _ in (0,)]
                mskm = [[0] * (KT_of[nq] // 2) for nq in range(NQ)]
                rec_m = [[0] * 4 for _ in range(NQ)]
                ts_m = [[0] * 4 for _ in range(NQ)]
                xe_m = [[0] * 4 for _ in range(NQ)]
                cnt = vcb + 1
                ones_m = cnt
                for u_ in range(NU):
                    for r in range(NR_of[u_]):
                        cnt += 1; vte_m[u_][r] = cnt
                    for g in range(4):
                        nq = u_ * 4 + g
                        for p in range(KT_of[nq] // 2):
                            cnt += 1; mskm[nq][p] = cnt
                        for i in range(4):
                            cnt += 1; rec_m[nq][i] = cnt
                            cnt += 1; ts_m[nq][i] = cnt
                        for i in range(4):
                            cnt += 1; xe_m[nq][i] = cnt

                def emit_S(t, nq, kt):
                    sb2 = (kt // 2) % 2
                    half = (kt % 2) * 512
                    if kt >= 4:
                        ACh.w(t, exp_m[nq][kt // 2 - 2])
                    elif nq >= 1:
                        pp = KT_of[nq - 1] // 2 - 2 + kt // 2
                        ACh.w(t, exp_m[nq - 1][pp])
                    PEs.inc(t.matmul(
                        ps_S[sb2][:, half:half + 512],
                        lhsT=k_sb[:, kt * 128:(kt + 1) * 128],
                        rhs=q_sb[nq % 2][:],
                        start=True, stop=True), 1, expect=S_m[nq][kt])

                def emit_PV(t, nq, kt):
                    KT = KT_of[nq]
                    sb4 = (kt // 2) % 4
                    half = (kt % 2) * 512
                    VC.w(t, mskm[nq][kt // 2])
                    if kt == 0 and nq >= 1:
                        VC.w(t, ts_m[nq - 1][3])
                    for i in range(4):
                        bk, off = i // 2, (i % 2) * 256
                        ins = t.matmul(
                            ps_C[bk][:, off:off + 129],
                            lhsT=p_sb[sb4][:, half + i * 128:half + (i + 1) * 128],
                            rhs=vt_b[:].rearrange(
                                "p (kk c) -> p kk c", kk=32)[:, kt, 0:129],
                            start=(kt == 0 and i % 2 == 0), stop=(kt == KT - 1))
                    PEs.inc(ins, 1, expect=PV_m[nq][kt])

                @blk.sync
                def _(sy):
                    sm["cc"].w(sy)
                    sm["evst"].w(sy)
                    nq = 0
                    for qr in range(4):
                        if qr >= 1:
                            VC.w(sy, mskm[qr * 32 - 1][-1])
                        src = maskT[:, qr * 512:(qr + 1) * 512].rearrange(
                            "(c p) t -> p c t", p=128)
                        sm["msk"].inc(sy.dma_start(out=msk_sb[:].rearrange(
                            "p (c t) -> p c t", c=NF), in_=src), 16)
                        for kvh in range(8):
                            u_ = qr * 8 + kvh
                            if u_ >= 1:
                                PEs.w(sy, PV_m[u_ * 4 - 1][-1])
                                TRs.w(sy, vtr_m[u_ - 1][NR_of[u_ - 1] - 1])
                            for half in range(2):
                                sm["qk"].inc(sy.dma_start(
                                    out=k_sb[:, half * T:(half + 1) * T],
                                    in_=kvb_gath[half * T + kvh * 128:
                                                 half * T + (kvh + 1) * 128, :]), 16)
                                sm["qk"].inc(sy.dma_start(
                                    out=v_sb[:, half * T:(half + 1) * T],
                                    in_=kvb_gath[half * T + 1024 + kvh * 128:
                                                 half * T + 1024 + (kvh + 1) * 128, :]), 16)
                            for g in range(4):
                                h = kvh * 4 + g
                                if nq >= 2:
                                    PEs.w(sy, PV_m[nq - 2][-1])
                                sm["aQ"].inc(sy.dma_start(
                                    out=q_sb[nq % 2][:],
                                    in_=qb_dram[h * 128:(h + 1) * 128,
                                                qr * 512:(qr + 1) * 512]), 16)
                                nq += 1

                @blk.tensor
                def _(t):
                    nq = 0
                    for qr in range(4):
                        for kvh in range(8):
                            u_ = qr * 8 + kvh
                            t.wait_ge(sm["qk"].h, qkb + 64 * (u_ + 1))
                            for r in range(NR_of[u_]):
                                if u_ + r >= 1:
                                    if r >= 1:
                                        VC.w(t, vte_m[u_][r - 1])
                                    else:
                                        VC.w(t, vte_m[u_ - 1][NR_of[u_ - 1] - 1])
                                for c8 in range(8):
                                    kt = r * 8 + c8
                                    ins = t.transpose(
                                        ps_V[:, c8 * 128:(c8 + 1) * 128],
                                        v_sb[:, kt * 128:(kt + 1) * 128], id_bf[:])
                                TRs.inc(ins, 1, expect=vtr_m[u_][r])
                            for g in range(4):
                                KT = KT_of[nq]
                                P = KT // 2
                                t.wait_ge(sm["aQ"].h, qb0 + 16 * (nq + 1))
                                VC.w(t, vte_m[u_][NR_of[u_] - 1])
                                VC.w(t, ones_m)
                                for kt in range(4):
                                    emit_S(t, nq, kt)
                                for p in range(P):
                                    emit_PV(t, nq, 2 * p)
                                    emit_PV(t, nq, 2 * p + 1)
                                    if p < P - 2:
                                        emit_S(t, nq, 2 * p + 4)
                                        emit_S(t, nq, 2 * p + 5)
                                VC.w(t, ts_m[nq][3])
                                if nq >= 1:
                                    VC.w(t, xe_m[nq - 1][3])
                                for i in range(4):
                                    ins = t.transpose(ps_X[:, i * 128:(i + 1) * 128],
                                                      cstg[:, i * 128:(i + 1) * 128],
                                                      id_bf[:])
                                    TRs.inc(ins, 1, expect=ctr_m[nq][i])
                                nq += 1

                @blk.scalar
                def _(s):
                    for nq in range(NQ):
                        for p in range(KT_of[nq] // 2):
                            sb2 = p % 2
                            PEs.w(s, S_m[nq][2 * p + 1])
                            if p >= 4:
                                PEs.w(s, PV_m[nq][2 * p - 7])
                            elif nq >= 1:
                                PEs.w(s, PV_m[nq - 1][-1])
                            ACh.inc(s.activation(p_sb[p % 4][:], ps_S[sb2][:],
                                                 AF.Exp, bias=c0b[:],
                                                 scale=ISQ), 1,
                                    expect=exp_m[nq][p])

                @blk.vector
                def _(v):
                    VC.inc(v.memset(vt_b[:].rearrange(
                        "p (kk c) -> p kk c", kk=32)[:, :, 128:129], 1.0), 1,
                        expect=ones_m)
                    nq = 0
                    for qr in range(4):
                        v.wait_ge(sm["msk"].h, mb + 16 * (qr + 1))
                        for kvh in range(8):
                            u_ = qr * 8 + kvh
                            if u_ >= 1:
                                PEs.w(v, PV_m[u_ * 4 - 1][-1])
                            for r in range(NR_of[u_]):
                                TRs.w(v, vtr_m[u_][r])
                                dst_v = vt_b[:].rearrange("p (kk c) -> p kk c", kk=32)[
                                    :, r * 8:(r + 1) * 8, 0:128]
                                VC.inc(v.tensor_copy(dst_v, cv(ps_V[:], c=8)), 1,
                                       expect=vte_m[u_][r])
                            for g in range(4):
                                for p in range(KT_of[nq] // 2):
                                    ACh.w(v, exp_m[nq][p])
                                    VC.inc(v.tensor_tensor(
                                        p_sb[p % 4][:].rearrange(
                                            "p (a t) -> p a t", a=2),
                                        p_sb[p % 4][:].rearrange(
                                            "p (a t) -> p a t", a=2),
                                        cv(msk_sb[:])[:, 2 * p:2 * p + 2, :],
                                        OP.mult), 1, expect=mskm[nq][p])
                                PEs.w(v, PV_m[nq][-1])
                                if nq >= 2:
                                    sm["ust"].w(v, stb + 16 * (nq - 1))
                                if nq >= 1:
                                    TRs.w(v, ctr_m[nq - 1][3])
                                for i in range(4):
                                    bk, off = i // 2, (i % 2) * 256
                                    VC.inc(v.reciprocal(
                                        rzb[:], ps_C[bk][:, off + 128:off + 129]), 1,
                                        expect=rec_m[nq][i])
                                    VC.w(v, rec_m[nq][i])
                                    VC.inc(v.tensor_scalar(
                                        cstg[:, i * 128:(i + 1) * 128],
                                        ps_C[bk][:, off:off + 128],
                                        rzb[:], None, OP.mult), 1,
                                        expect=ts_m[nq][i])
                                TRs.w(v, ctr_m[nq][3])
                                for i in range(4):
                                    VC.inc(v.tensor_copy(
                                        ctstg[nq % 2][:, i * 128:(i + 1) * 128],
                                        ps_X[:, i * 128:(i + 1) * 128]), 1,
                                        expect=xe_m[nq][i])
                                nq += 1

                @blk.gpsimd
                def _(g):
                    nq = 0
                    for qr in range(4):
                        for kvh in range(8):
                            for gg in range(4):
                                h = kvh * 4 + gg
                                VC.w(g, xe_m[nq][3])
                                sm["ust"].inc(g.dma_start(
                                    out=ctxbT[h * 128:(h + 1) * 128,
                                              qr * 512:(qr + 1) * 512],
                                    in_=ctstg[nq % 2][:]), 16)
                                nq += 1

        sub_ctx = ExitStack()
        xnT = sub_ctx.enter_context(nc.sbuf_tensor("xnT1", [128, NF * T], BF16))
        emit_boundary(1, xnT)

        def slab_jobs(slab_ap, bias_sb_):
            jobs = []
            for oc in range(NF):
                stores = [(slab_ap[sl][oc * 128:(oc + 1) * 128, :],
                           sl * 512, (sl + 1) * 512) for sl in range(NSLAB)]
                jobs.append(dict(col=oc,
                                 bias=(bias_sb_[:, oc:oc + 1]
                                       if bias_sb_ is not None else None),
                                 stores=stores))
            return jobs

        emit_proj(wq_t, slab_jobs(qa_slab, bq_sb), "q", xnT)
        emit_proj(wk_t, slab_jobs(ka_slab, bk_sb), "k", xnT)
        emit_proj(wv_t, slab_jobs(va_slab, bv_sb), "v", xnT)
        sub_ctx.close()
        emit_mix()
        if n_sub >= 2:
            sub_ctx = ExitStack()
            xnT = sub_ctx.enter_context(nc.sbuf_tensor("xnT2", [128, NF * T], BF16))
            emit_boundary(2, xnT)
            emit_fmout("dequant", "f1", xnT, bias_sb=fb_sb)
            sub_ctx.close()
        if n_sub >= 3:
            sub_ctx = ExitStack()
            xnT = sub_ctx.enter_context(nc.sbuf_tensor("xnT3", [128, NF * T], BF16))
            emit_boundary(3, xnT)
            kv_jobs = [dict(col=oc, bias=None,
                            stores=[(kvb_loc[(oc - 32) * 128:(oc - 31) * 128, :], 0, T)])
                       for oc in range(32, 48)]
            emit_proj(wqkv_t, kv_jobs, "kv", xnT)
            emit_collective()
            q_jobs = [dict(col=oc, bias=None,
                           stores=[(qb_dram[oc * 128:(oc + 1) * 128, :], 0, T)])
                      for oc in range(32)]
            emit_proj(wqkv_t, q_jobs, "qb", xnT)
            sub_ctx.close()
            if "attb" not in os.environ.get("KSKIP", ""):
                emit_attb()
            sub_ctx = ExitStack()
            xnT = sub_ctx.enter_context(nc.sbuf_tensor("ctxT", [128, NF * T], BF16))
            with nc.Block() as blk:
                @blk.sync
                def _(sy):
                    sm["ust"].w(sy)
                    src = ctxbT.rearrange("(c p) t -> p c t", p=128)
                    sm["qk"].inc(sy.dma_start(out=cv(xnT[:]), in_=src), 16)
                @blk.tensor
                def _(t):
                    sm["qk"].w(t)
            if "dense" not in os.environ.get("KSKIP", ""):
                emit_fmout("w32", "dn", xnT, w_ap=wd_t, bias_sb=None)
            sub_ctx.close()
        if n_sub >= 4:
            sub_ctx = ExitStack()
            xnT = sub_ctx.enter_context(nc.sbuf_tensor("xnT4", [128, NF * T], BF16))
            emit_boundary(4, xnT)
            emit_fmout("cache", "f2", xnT)
            sub_ctx.close()
        if dbg:
            with nc.Block() as blk:
                @blk.sync
                def _(sy):
                    sm["ust"].w(sy)
                    sm["cc"].w(sy)
                    sm["aux"].inc(sy.dma_start(out=dbg["d_gath"], in_=kvb_gath), 16)
                    sm["aux"].inc(sy.dma_start(out=dbg["d_ctxb"], in_=ctxbT), 16)
                    sm["aux"].inc(sy.dma_start(out=dbg["d_qb"], in_=qb_dram), 16)
                    sm["aux"].inc(sy.dma_start(out=dbg["d_u"], in_=u_dram), 16)
                    sm["aux"].w(sy)
        emit_boundary(n_sub + 1)

    return nc


def _prep_shared(inputs):
    def tr(w):
        return np.ascontiguousarray(w.astype(np.float32, copy=False).T)
    sh = {}
    sh["wq_t"] = tr(inputs["L0_attn_a_wq"])
    sh["wk_t"] = tr(inputs["L0_attn_a_wk"])
    sh["wv_t"] = tr(inputs["L0_attn_a_wv"])
    sh["bq_r"] = np.ascontiguousarray(
        inputs["L0_attn_a_bq"].astype(np.float32).reshape(NF, 128).T)
    sh["bk_r"] = np.ascontiguousarray(
        inputs["L0_attn_a_bk"].astype(np.float32).reshape(NF, 128).T)
    sh["bv_r"] = np.ascontiguousarray(
        inputs["L0_attn_a_bv"].astype(np.float32).reshape(NF, 128).T)
    sh["wqkv_t"] = tr(inputs["L0_attn_b_w_qkv"])
    sh["wd_t"] = tr(inputs["L0_attn_b_w_dense"])
    sh["qv_t"] = np.ascontiguousarray(
        inputs["L0_ffn_q_vals"].transpose(1, 2, 0).reshape(D, D)).astype(np.int32)
    sh["qs_t"] = np.ascontiguousarray(inputs["L0_ffn_q_scales"].astype(np.float32).T)
    sh["fb_r"] = np.ascontiguousarray(
        inputs["L0_ffn_bias"].astype(np.float32).reshape(NF, 128).T)
    tk = np.arange(2 * T)[:, None]
    sh["_masks"] = {}
    for half in range(2):
        tq = half * T + np.arange(T)[None, :]
        sh["_masks"][half] = np.ascontiguousarray((tk <= tq).astype(BFNP))
    return sh


_dbg_results = None


def kernel(n_sub=4, **inputs):
    _ = inputs.get("L0_attn_a_wl"), inputs.get("L0_attn_a_bl")
    sh = _prep_shared(inputs)
    in_maps = []
    for c in range(8):
        b, half = c // 2, c % 2
        m = dict(x0=np.ascontiguousarray(
            inputs["L0_attn_a_hidden_states"][b, half * T:(half + 1) * T, :]
        ).astype(np.float32, copy=False))
        for k2, v2 in sh.items():
            if k2 == "_masks":
                m["maskT"] = sh["_masks"][half]
            else:
                m[k2] = v2
        in_maps.append(m)
    nc = build(n_sub=n_sub)
    res = run_bass_kernel_spmd(nc, in_maps, core_ids=list(range(8)))
    out = np.empty((4, 4096, 4096), np.float32)
    for c in range(8):
        b, half = c // 2, c % 2
        out[b, half * T:(half + 1) * T, :] = res.results[c]["y"]
    if os.environ.get("KDBG"):
        global _dbg_results
        _dbg_results = res.results
    return out
